# version 16
# baseline (speedup 1.0000x reference)
import numpy as np
import ml_dtypes
import concourse.bass as bass
import concourse.mybir as mybir
from concourse.bass_utils import run_bass_kernel_spmd
from contextlib import ExitStack

F32 = mybir.dt.float32
BF16 = mybir.dt.bfloat16
ACT = mybir.ActivationFunctionType
ALU = mybir.AluOpType
BF = ml_dtypes.bfloat16

NCORE = 8
TOWN = 2048
NT = 18
NSUB = 17
D = 1024
NEG = -1.0e4
TINY = 1e-12

DEBUG_TAPS = ()


class Sched:
    ENGS = ("tensor", "vector", "scalar", "sync", "gpsimd")
    ENG_SEM = {"tensor": "pe", "vector": "vv", "scalar": "ss",
               "sync": "sy", "gpsimd": "gp"}

    POOLS = {"dx": 8, "dg": 16, "st": 16}

    def __init__(self, nc, stack):
        self.nc = nc
        self.plan = {e: [] for e in self.ENGS}
        names = ["pe", "vv", "ss", "sy", "gp", "cc"]
        for q, np_ in self.POOLS.items():
            names += ["%s%d" % (q, j) for j in range(np_)]
        self.sems = {n: stack.enter_context(nc.semaphore(n)) for n in names}
        self.cnt = {n: 0 for n in names}
        self.rr = {q: 0 for q in self.POOLS}
        self.lastw = {}
        self.marks = {e: 0 for e in self.ENGS}
        self.sem_name = {self.sems[n]: n for n in names}

    def op(self, eng, meth, *args, **kw):
        self.plan[eng].append({"m": meth, "a": args, "k": kw, "i": []})

    def fence(self, eng, name=None):
        items = self.plan[eng]
        for i in range(len(items) - 1, -1, -1):
            it = items[i]
            if it["m"] == "wait_ge":
                continue
            if it["i"]:
                sem, k = it["i"][-1]
                n = self.sem_name[sem]
                return (n, self.cnt[n])
            n = name or self.ENG_SEM[eng]
            k = 16 if it["m"].startswith("dma") else 1
            self.cnt[n] += k
            it["i"].append((self.sems[n], k))
            return (n, self.cnt[n])
        raise RuntimeError(f"no instruction to fence on {eng}")

    def dma(self, eng, out, in_, queue):
        n = "%s%d" % (queue, self.rr[queue] % self.POOLS[queue])
        self.rr[queue] += 1
        self.cnt[n] += 16
        self.plan[eng].append({"m": "dma_start", "a": (),
                               "k": {"out": out, "in_": in_},
                               "i": [(self.sems[n], 16)]})
        return (n, self.cnt[n])

    def wait_queue(self, eng, queue):
        for j in range(self.POOLS[queue]):
            self.wait(eng, ("%s%d" % (queue, j), self.cnt["%s%d" % (queue, j)]))

    def wait(self, eng, tok):
        name, val = tok
        if val <= 0:
            return
        key = (eng, name)
        if self.lastw.get(key, 0) >= val:
            return
        self.lastw[key] = val
        self.plan[eng].append({"m": "wait_ge", "a": (self.sems[name], val),
                               "k": {}, "i": []})

    def barrier(self):
        fences = []
        for eng in self.ENGS:
            if any(it["m"] != "wait_ge" for it in self.plan[eng][self.marks[eng]:]):
                fences.append(self.fence(eng))
        for q, np_ in self.POOLS.items():
            for j in range(np_):
                n = "%s%d" % (q, j)
                if self.cnt[n]:
                    fences.append((n, self.cnt[n]))
        fences.append(("cc", self.cnt["cc"]))
        for eng in self.ENGS:
            for tok in fences:
                self.wait(eng, tok)
        for eng in self.ENGS:
            self.marks[eng] = len(self.plan[eng])

    def wait_all(self, eng, toks):
        for tok in toks:
            self.wait(eng, tok)

    def replay(self, block):
        def mk(eng):
            def run(e):
                for it in self.plan[eng]:
                    r = getattr(e, it["m"])(*it["a"], **it["k"])
                    for sem, k in it["i"]:
                        r.then_inc(sem, k)
            return run
        for eng in self.ENGS:
            getattr(block, eng)(mk(eng))


def build_program(debug_taps=(), stop=None, skip=frozenset()):
    nc = bass.Bass("TRN2", target_bir_lowering=False, detect_race_conditions=False)

    def inp(name, shape, dt=F32):
        return nc.dram_tensor(name, shape, dt, kind="ExternalInput").ap()

    X = inp("x", [NT * 128, D])
    QKV_W = inp("qkv_w", [D, 3 * D])
    QKV_B = inp("qkv_b", [1, 3 * D])
    RQ_W = inp("rq_w", [D, 256])
    RQ_B = inp("rq_b", [1, 256])
    RKV_W = inp("rkv_w", [D, 512])
    RKV_B = inp("rkv_b", [1, 512])
    ROUT_W = inp("rout_w", [256, D])
    ROUT_B = inp("rout_b", [1, D])
    OUT_W = inp("out_w", [D, D])
    OUT_B = inp("out_b", [1, D])
    GL_W = inp("gl_w", [64, 1])
    GR_W = inp("gr_w", [64, 1])
    W_R = inp("W_r", [D, D])
    W_K = inp("W_k", [D, D])
    W_V = inp("W_v", [D, D])
    MU_R = inp("mu_r", [128, 8])
    MU_K = inp("mu_k", [128, 8])
    IDENT = inp("ident", [128, 128], BF16)
    IDENTF = inp("identf", [128, 128], F32)
    TRI = inp("tri", [128, 1024], BF16)
    ESUM = inp("esum", [128, 256], BF16)
    EBC = inp("ebc", [16, 1024], BF16)
    NEGALL = inp("negall", [1, 640], BF16)
    LATNEG = inp("latneg", [1, 576], BF16)
    HKEEP = inp("hkeep", [128, 1], F32)

    OUT = nc.dram_tensor("out", [TOWN, D], F32, kind="ExternalOutput").ap()

    wT_d = [nc.dram_tensor(n, [D, D], BF16, kind="Internal")
            for n in ("wrT_d", "wkT_d", "wvT_d")]
    qkv_bf = nc.dram_tensor("qkv_bf", [D, 3 * D], BF16, kind="Internal")
    rq_bf = nc.dram_tensor("rq_bf", [D, 256], BF16, kind="Internal")
    rkv_bf = nc.dram_tensor("rkv_bf", [D, 512], BF16, kind="Internal")
    rout_bf = nc.dram_tensor("rout_bf", [256, D], BF16, kind="Internal")
    out_bf = nc.dram_tensor("out_bf", [D, D], BF16, kind="Internal")
    cc_in = nc.dram_tensor("cc_in", [8192], F32, kind="Internal")
    cc_out = nc.dram_tensor("cc_out", [8, 8192], F32, kind="Internal")

    dbg_shapes = {"x1": [NT * 128, D], "x2": [NT * 128, D],
                  "y1fm": [128, 8 * 2305], "lnstat": [128, 5 * NT],
                  "y2fm": [128, 8 * 2304], "pooled": [128, 512],
                  "y3fm": [128, 8 * 2305],
                  "osb": [128, NSUB * 1024], "rsb": [128, NSUB * 1024],
                  "gate": [16, NSUB * 128]}
    dbg = {t: nc.dram_tensor("dbg_" + t, dbg_shapes[t], F32,
                             kind="ExternalOutput").ap()
           for t in debug_taps}

    _uc = [0]

    def U(n):
        _uc[0] += 1
        return "%s_u%d" % (n, _uc[0])

    with ExitStack() as STK:
        E = STK.enter_context
        sc = Sched(nc, STK)

        x_sb = E(nc.sbuf_tensor("x_sb", [128, NT, D], F32))
        ident = E(nc.sbuf_tensor("ident_s", [128, 128], BF16))
        mur = E(nc.sbuf_tensor("mur_s", [128, 8], F32))
        muk = E(nc.sbuf_tensor("muk_s", [128, 8], F32))
        onesb = E(nc.sbuf_tensor("onesb", [1, 1024], BF16))
        onescol = E(nc.sbuf_tensor("onescol", [1, 128], BF16))
        zrow = E(nc.sbuf_tensor("zrow", [1, 128], BF16))
        epsv = E(nc.sbuf_tensor("epsv", [128, 1], F32))
        junk = E(nc.sbuf_tensor("junk", [128, 132], F32))
        tinyv = E(nc.sbuf_tensor("tinyv", [128, 1], F32))
        ln_sums = E(nc.sbuf_tensor("ln_sums", [128, NT], F32))
        ln_sq = E(nc.sbuf_tensor("ln_sq", [128, NT], F32))
        ln_var = E(nc.sbuf_tensor("ln_var", [128, NT], F32))
        ln_inv = E(nc.sbuf_tensor("ln_inv", [128, NT], F32))
        ln_nmi = E(nc.sbuf_tensor("ln_nmi", [128, NT], F32))

        def vgap():
            sc.op("vector", "tensor_copy", junk[:, 64:128], junk[:, 0:64])

        tok_x = sc.dma("sync", x_sb[:, :, :],
                       X.rearrange("(s p) d -> p s d", p=128), "dx")
        cons_toks = {}
        for dst, srcx in ((ident, IDENT), (mur, MU_R), (muk, MU_K)):
            cons_toks[dst.name] = sc.dma("sync", dst[:, :], srcx, "dx")
        sc.op("vector", "memset", junk[:, :], 0.0)
        sc.op("vector", "memset", onesb[:, :], 1.0)
        sc.op("vector", "memset", onescol[:, :], 1.0)
        sc.op("vector", "memset", zrow[:, :], 0.0)
        sc.op("vector", "memset", epsv[:, :], 1e-5)
        sc.op("vector", "memset", tinyv[:, :], TINY)


        def ln_sweep(tiles, dst, col0):
            sl, sr = tiles[0], tiles[-1] + 1
            with nc.sbuf_tensor("lns_%d_%d" % (sl, col0), [128, 2, D], F32) as scr, \
                 nc.sbuf_tensor("lnv_%d_%d" % (sl, col0), [128, 2, D], F32) as scr2, \
                 nc.sbuf_tensor("lny_%d_%d" % (sl, col0), [128, 2, D], BF16) as yscr:
                for i, s_ in enumerate(tiles):
                    b = i % 2
                    sc.op("scalar", "activation", scr[:, b, :], x_sb[:, s_, :],
                          ACT.Square, accum_out=ln_sq[:, s_:s_ + 1])
                    sc.op("vector", "tensor_scalar", scr2[:, b, :],
                          x_sb[:, s_, :], 1.0, 0.0, ALU.mult, op1=ALU.add,
                          accum_out=ln_sums[:, s_:s_ + 1])
                sc.op("scalar", "copy", junk[:, 0:1], epsv[:, :])
                tokS = sc.fence("scalar")
                sc.wait("vector", tokS)
                vgap()
                sc.op("vector", "tensor_scalar", ln_sums[:, sl:sr],
                      ln_sums[:, sl:sr], 1.0 / D, None, ALU.mult)
                sc.op("vector", "tensor_scalar", ln_sq[:, sl:sr],
                      ln_sq[:, sl:sr], 1.0 / D, None, ALU.mult)
                sc.op("vector", "scalar_tensor_tensor", ln_var[:, sl:sr],
                      ln_sums[:, sl:sr], -1.0, ln_sums[:, sl:sr],
                      op0=ALU.mult, op1=ALU.mult)
                vgap()
                sc.op("vector", "tensor_add", ln_var[:, sl:sr],
                      ln_var[:, sl:sr], ln_sq[:, sl:sr])
                vgap()
                sc.op("vector", "tensor_scalar", ln_var[:, sl:sr],
                      ln_var[:, sl:sr], 1e-5, None, ALU.add)
                tokV = sc.fence("vector")
                sc.wait("scalar", tokV)
                sc.op("scalar", "activation", ln_nmi[:, sl:sr],
                      ln_var[:, sl:sr], ACT.Sqrt)
                tokS = sc.fence("scalar")
                sc.wait("vector", tokS)
                sc.op("vector", "reciprocal", ln_inv[:, sl:sr],
                      ln_nmi[:, sl:sr])
                vgap()
                sc.op("vector", "tensor_mul", ln_nmi[:, sl:sr],
                      ln_inv[:, sl:sr], ln_inv[:, sl:sr])
                vgap()
                sc.op("vector", "tensor_mul", ln_nmi[:, sl:sr],
                      ln_nmi[:, sl:sr], ln_var[:, sl:sr])
                vgap()
                sc.op("vector", "tensor_scalar", ln_nmi[:, sl:sr],
                      ln_nmi[:, sl:sr], -0.5, 1.5, ALU.mult, op1=ALU.add)
                vgap()
                sc.op("vector", "tensor_mul", ln_inv[:, sl:sr],
                      ln_inv[:, sl:sr], ln_nmi[:, sl:sr])
                vgap()
                sc.op("vector", "scalar_tensor_tensor", ln_nmi[:, sl:sr],
                      ln_sums[:, sl:sr], -1.0, ln_inv[:, sl:sr],
                      op0=ALU.mult, op1=ALU.mult)
                tokV = sc.fence("vector")
                sc.wait("scalar", tokV)
                ev = {}
                with nc.psum_tensor(U("ps_ln0"), [128, 1024], BF16) as p0, \
                     nc.psum_tensor(U("ps_ln1"), [128, 1024], BF16) as p1:
                    pls = [p0, p1]
                    for i, s_ in enumerate(tiles):
                        b = i % 2
                        if i >= 2:
                            sc.wait("scalar", ev[i - 2])
                        sc.op("scalar", "activation", yscr[:, b, :],
                              x_sb[:, s_, :], ACT.Identity,
                              bias=ln_nmi[:, s_:s_ + 1], scale=ln_inv[:, s_:s_ + 1])
                        tokS = sc.fence("scalar")
                        sc.wait("tensor", tokS)
                        if i >= 2:
                            sc.wait("tensor", ev[i - 2])
                        for c in range(8):
                            sc.op("tensor", "transpose",
                                  pls[b][:, 128 * c:128 * c + 128],
                                  yscr[:, b, 128 * c:128 * c + 128], ident[:, :])
                        tokP = sc.fence("tensor")
                        sc.wait("vector", tokP)
                        col = col0 + 128 * i
                        sc.op("vector", "tensor_copy", dst[:, :, col:col + 128],
                              pls[b][:, :].rearrange("p (c q) -> p c q", c=8))
                        ev[i] = sc.fence("vector")
            sc.barrier()

        def ffn_sweep(yw, col0, ncols, tile0, wt_sb, store=False,
                      tokWs_pre=None):
            if tokWs_pre is not None:
                tokWs = tokWs_pre
            else:
                tokWs = []
                for wi in range(3):
                    tokWs.append(sc.dma("gpsimd", wt_sb[:, 8 * wi:8 * wi + 8, :],
                                 wT_d[wi].ap().rearrange("(c p) d -> p c d", p=128),
                                 "dg"))
            with nc.sbuf_tensor("fxr%d" % col0, [128, 8, 512], BF16) as xr_b, \
                 nc.sbuf_tensor("fxk%d" % col0, [128, 8, 512], BF16) as xk_b, \
                 nc.sbuf_tensor("fka%d" % col0, [128, 8, 512], BF16) as kact_b, \
                 nc.sbuf_tensor("fkr%d" % col0, [128, 512], F32) as krelu, \
                 nc.sbuf_tensor("frs%d" % col0, [128, D], F32) as r_scr, \
                 nc.sbuf_tensor("fr2%d" % col0, [128, D], F32) as r_scr2:
                nblk = (ncols + 511) // 512
                subs_done = set()

                def emit_subs(b_):
                    c0b = col0 + 512 * b_
                    wb = min(512, ncols - 512 * b_)
                    for c in range(8):
                        sc.op("vector", "tensor_sub", xk_b[:, c, 0:wb],
                              yw[:, c, c0b:c0b + wb],
                              yw[:, c, c0b - 1:c0b - 1 + wb])
                for blk in range(nblk):
                    c0 = col0 + 512 * blk
                    w = min(512, ncols - 512 * blk)
                    sc.wait_all("vector", tokWs)
                    if blk not in subs_done:
                        emit_subs(blk)
                    mixF = {}
                    for c in range(8):
                        sc.op("vector", "scalar_tensor_tensor", xr_b[:, c, 0:w],
                              xk_b[:, c, 0:w], mur[:, c:c + 1],
                              yw[:, c, c0 - 1:c0 - 1 + w],
                              op0=ALU.mult, op1=ALU.add)
                        sc.op("vector", "scalar_tensor_tensor", xk_b[:, c, 0:w],
                              xk_b[:, c, 0:w], muk[:, c:c + 1],
                              yw[:, c, c0 - 1:c0 - 1 + w],
                              op0=ALU.mult, op1=ALU.add)
                        mixF[c] = sc.fence("vector")
                    with nc.psum_tensor(U("ps_k0"), [128, 512], F32) as k0, \
                         nc.psum_tensor(U("ps_k1"), [128, 512], F32) as k1:
                        pks = [k0, k1]
                        kq = {}
                        for dch in range(8):
                            b = dch % 2
                            if dch >= 2:
                                sc.wait("tensor", kq[dch - 2])
                            for cch in range(8):
                                if dch == 0:
                                    sc.wait("tensor", mixF[cch])
                                sc.op("tensor", "matmul", pks[b][:, 0:w],
                                      wt_sb[:, 8 + cch, 128 * dch:128 * dch + 128],
                                      xk_b[:, cch, 0:w],
                                      start=(cch == 0), stop=(cch == 7))
                            tokP = sc.fence("tensor")
                            sc.wait("scalar", tokP)
                            sc.op("scalar", "activation", krelu[:, 0:w],
                                  pks[b][:, 0:w], ACT.Relu)
                            sc.op("scalar", "activation", kact_b[:, dch, 0:w],
                                  krelu[:, 0:w], ACT.Square)
                            kq[dch] = sc.fence("scalar")
                        tok_kact = kq[7]
                    if blk + 1 < nblk:
                        sc.wait("vector", kq[7])
                        emit_subs(blk + 1)
                        subs_done.add(blk + 1)
                    ntile = (w + 127) // 128
                    with nc.psum_tensor(U("ps_r0"), [128, 1024], F32) as pr0, \
                         nc.psum_tensor(U("ps_r1"), [128, 1024], F32) as pr1, \
                         nc.psum_tensor(U("ps_v0"), [128, 1024], F32) as pv0, \
                         nc.psum_tensor(U("ps_v1"), [128, 1024], F32) as pv1:
                        prs = [pr0, pr1]
                        pvs = [pv0, pv1]
                        rsb = [r_scr, r_scr2]
                        hist = {}
                        for ti in range(ntile):
                            b2 = ti % 2
                            pr, pv = prs[b2], pvs[b2]
                            rsc = rsb[b2]
                            tw = min(128, w - 128 * ti)
                            s_out = tile0 + 4 * blk + ti
                            sc.wait("tensor", tok_kact)
                            if ti >= 2:
                                sc.wait("tensor", hist[ti - 2][0])
                                sc.wait("tensor", hist[ti - 2][1])
                            for half in range(2):
                                for cch in range(8):
                                    sc.op("tensor", "matmul",
                                          pr[0:tw, 512 * half:512 * half + 512],
                                          xr_b[:, cch, 128 * ti:128 * ti + tw],
                                          wt_sb[:, cch, 512 * half:512 * half + 512],
                                          start=(cch == 0), stop=(cch == 7))
                            tokR = sc.fence("tensor")
                            for half in range(2):
                                for cch in range(8):
                                    sc.op("tensor", "matmul",
                                          pv[0:tw, 512 * half:512 * half + 512],
                                          kact_b[:, cch, 128 * ti:128 * ti + tw],
                                          wt_sb[:, 16 + cch, 512 * half:512 * half + 512],
                                          start=(cch == 0), stop=(cch == 7))
                            tokV2 = sc.fence("tensor")
                            sc.wait("scalar", tokR)
                            sc.op("scalar", "activation", rsc[0:tw, :],
                                  pr[0:tw, :], ACT.Sigmoid)
                            tokS = sc.fence("scalar")
                            sc.wait("vector", tokV2)
                            sc.wait("vector", tokS)
                            sc.op("vector", "tensor_mul", rsc[0:tw, :],
                                  rsc[0:tw, :], pv[0:tw, :])
                            sc.op("vector", "tensor_add", x_sb[0:tw, s_out, :],
                                  x_sb[0:tw, s_out, :], rsc[0:tw, :])
                            tokRes = sc.fence("vector")
                            hist[ti] = (tokS, tokRes)
                            if store:
                                sc.wait("sync", tokRes)
                                sc.dma("sync",
                                       OUT[128 * (s_out - 2):128 * (s_out - 2) + tw, :],
                                       x_sb[0:tw, s_out, :], "st")
                        sc.wait("tensor", hist[ntile - 1][0])
                        sc.wait("tensor", hist[ntile - 1][1])
                        if ntile >= 2:
                            sc.wait("tensor", hist[ntile - 2][0])
                            sc.wait("tensor", hist[ntile - 2][1])
            sc.barrier()

        def early_finish():
            sc.barrier()
            for s_out in range(2, NT):
                sc.dma("sync", OUT[128 * (s_out - 2):128 * (s_out - 2) + 128, :],
                       x_sb[:, s_out, :], "st")
            sc.wait_queue("sync", "st")
            with nc.Block() as block:
                sc.replay(block)
            return nc

        with nc.sbuf_tensor("yw1", [128, 8, 2305], BF16) as yw1, \
             nc.sbuf_tensor("wt_sb1", [128, 24, 1024], BF16) as wt_sb1, \
             nc.sbuf_tensor("wload", [128, 2, D], BF16) as wload:
            with nc.psum_tensor(U("ps_wt0"), [128, 1024], BF16) as pw0, \
                 nc.psum_tensor(U("ps_wt1"), [128, 1024], BF16) as pw1:
                pwt = [pw0, pw1]
                ev_hist, pe_hist = {}, {}
                for wi, (W, WT) in enumerate(((W_R, wT_d[0]), (W_K, wT_d[1]),
                                              (W_V, wT_d[2]))):
                    for dch in range(8):
                        i = wi * 8 + dch
                        b = i % 2
                        if i >= 2:
                            sc.wait("gpsimd", pe_hist[i - 2])
                        tokL = sc.dma("gpsimd", wload[:, b, :],
                                      W[128 * dch:128 * dch + 128, :], "dg")
                        sc.wait("tensor", tokL)
                        sc.wait("tensor", cons_toks["ident_s"])
                        if i >= 2:
                            sc.wait("tensor", ev_hist[i - 2])
                        for cch in range(8):
                            sc.op("tensor", "transpose",
                                  pwt[b][:, 128 * cch:128 * cch + 128],
                                  wload[:, b, 128 * cch:128 * cch + 128],
                                  ident[:, :])
                        pe_hist[i] = sc.fence("tensor")
                        sc.wait("vector", pe_hist[i])
                        wdst = wt_sb1[:, 8 * wi:8 * wi + 8,
                                      128 * dch:128 * dch + 128]
                        if wi == 2:
                            sc.op("vector", "tensor_scalar", wdst,
                                  pwt[b][:, :].rearrange("p (c q) -> p c q",
                                                         c=8),
                                  0.5, None, ALU.mult)
                        else:
                            sc.op("vector", "tensor_copy", wdst,
                                  pwt[b][:, :].rearrange("p (c q) -> p c q",
                                                         c=8))
                        ev_hist[i] = sc.fence("vector")
                        sc.wait("sync", ev_hist[i])
                        sc.dma(
                            "sync",
                            WT.ap().rearrange("(c p) d -> p c d",
                                              p=128)[:, :, 128 * dch:128 * dch + 128],
                            wdst, "st")
                evlast = ev_hist[23]
                castW = {}
                for nm, bf, fp in (("qkv", qkv_bf, QKV_W),
                                   ("rq", rq_bf, RQ_W),
                                   ("rkv", rkv_bf, RKV_W),
                                   ("rout", rout_bf, ROUT_W),
                                   ("out", out_bf, OUT_W)):
                    castW[nm] = sc.dma("gpsimd", bf.ap(), fp, "dg")
            if stop == "wt":
                return early_finish()
            sc.wait_queue("scalar", "dx")
            sc.wait_queue("vector", "dx")
            sc.op("vector", "memset", yw1[:, :, 0:1], 0.0)
            ln_sweep(list(range(NT)), yw1, 1)
            if "lnstat" in dbg:
                for i2, tns in enumerate((ln_sums, ln_sq, ln_var, ln_inv,
                                          ln_nmi)):
                    sc.dma("sync", dbg["lnstat"][:, NT * i2:NT * i2 + NT],
                           tns[:, :], "st")
                sc.barrier()
            if "y1fm" in dbg:
                sc.dma("gpsimd", dbg["y1fm"].rearrange("p (c t) -> p c t", c=8),
                       yw1[:, :, :], "dg")
                sc.barrier()
            if stop == "ln1":
                return early_finish()
            ffn_sweep(yw1, 1, 2304, 0, wt_sb1, tokWs_pre=[evlast])
        if "x1" in dbg:
            sc.dma("sync", dbg["x1"].rearrange("(s p) d -> p s d", p=128),
                   x_sb[:, :, :], "st")
            sc.barrier()
        if stop == "ffn1":
            return early_finish()

        with ExitStack() as YTK:
            Y = YTK.enter_context
            y2_cm = Y(nc.sbuf_tensor("y2", [128, 8, 2304], BF16))
            tri = Y(nc.sbuf_tensor("tri_s", [128, 1024], BF16))
            esum = Y(nc.sbuf_tensor("esum_s", [128, 256], BF16))
            ebc = Y(nc.sbuf_tensor("ebc_s", [16, 1024], BF16))
            negall = Y(nc.sbuf_tensor("negall_s", [1, 640], BF16))
            latneg = Y(nc.sbuf_tensor("latneg_s", [1, 576], BF16))
            hkeep = Y(nc.sbuf_tensor("hkeep_s", [128, 1], F32))
            qkvb = Y(nc.sbuf_tensor("qkvb", [1, 3 * D], BF16))
            rqb = Y(nc.sbuf_tensor("rqb", [1, 256], BF16))
            rkvb = Y(nc.sbuf_tensor("rkvb", [1, 512], BF16))
            routb = Y(nc.sbuf_tensor("routb", [1, D], BF16))
            outb = Y(nc.sbuf_tensor("outb", [1, D], BF16))
            glG = Y(nc.sbuf_tensor("glG", [128, 128], BF16))
            grG = Y(nc.sbuf_tensor("grG", [128, 128], BF16))
            glv = Y(nc.sbuf_tensor("glv", [128, 1], BF16))
            grv = Y(nc.sbuf_tensor("grv", [128, 1], BF16))
            pooled_m = Y(nc.sbuf_tensor("pooled_m", [128, 64], F32))
            pooled_ab = Y(nc.sbuf_tensor("pooled_ab", [128, 512], BF16))
            rk_fm = Y(nc.sbuf_tensor("rk_fm", [128, 128], BF16))
            rv_lm = Y(nc.sbuf_tensor("rv_lm", [128, 256], BF16))
            for dst, srcx in ((tri, TRI), (esum, ESUM), (ebc, EBC),
                              (negall, NEGALL), (latneg, LATNEG),
                              (hkeep, HKEEP)):
                sc.dma("sync", dst[:, :], srcx, "dx")
            for dst, srcx in ((qkvb, QKV_B), (rqb, RQ_B), (rkvb, RKV_B),
                              (routb, ROUT_B), (outb, OUT_B)):
                sc.dma("gpsimd", dst[:, :], srcx, "dg")
            sc.dma("gpsimd", glv[0:64, :], GL_W, "dg")
            sc.dma("gpsimd", glv[64:128, :], GL_W, "dg")
            sc.dma("gpsimd", grv[0:64, :], GR_W, "dg")
            tok_small = sc.dma("gpsimd", grv[64:128, :], GR_W, "dg")
            sc.op("vector", "memset", glG[:, :], 0.0)
            sc.op("vector", "memset", grG[:, :], 0.0)
            sc.wait_queue("vector", "dg")
            for c in range(8):
                for G, gv in ((glG, glv), (grG, grv)):
                    sc.op("vector", "tensor_copy",
                          G[0:64, 16 * c + 2 * c:16 * c + 2 * c + 1],
                          gv[0:64, :])
                    sc.op("vector", "tensor_copy",
                          G[64:128, 16 * c + 2 * c + 1:16 * c + 2 * c + 2],
                          gv[64:128, :])
            ln_sweep(list(range(NT)), y2_cm, 0)
            if "y2fm" in dbg:
                sc.dma("gpsimd", dbg["y2fm"].rearrange("p (c t) -> p c t", c=8),
                       y2_cm[:, :, :], "dg")
                sc.barrier()

            with nc.sbuf_tensor("pool_scr", [128, 256], BF16) as pool_scr:
                for j in range(8):
                    for c in range(8):
                        sc.op("vector", "tensor_scalar", pool_scr[:, :],
                              y2_cm[:, c, 256 + 256 * j:512 + 256 * j],
                              1.0 / 256.0, 0.0, ALU.mult, op1=ALU.add,
                              accum_out=pooled_m[:, 8 * j + c:8 * j + c + 1])
                vgap()
                sc.op("vector", "tensor_copy", junk[:, 1:2], epsv[:, :])
                tokPool = sc.fence("vector")
            tokRe = None
            if stop == "pooled":
                return early_finish()

            with ExitStack() as ATK:
                A = ATK.enter_context
                q_grp = A(nc.sbuf_tensor("q_grp", [128, 8, 512], BF16))
                rq_grp = A(nc.sbuf_tensor("rq_grp", [128, 2, 512], BF16))
                k_ring = A(nc.sbuf_tensor("k_ring", [128, 8, 768], BF16))
                v_ring = A(nc.sbuf_tensor("v_ring", [128, 6, 1024], BF16))
                wbuf = A(nc.sbuf_tensor("wbuf", [128, 8, 512], BF16))
                wbuf2 = A(nc.sbuf_tensor("wbuf2", [128, 8, 512], BF16))
                routw = A(nc.sbuf_tensor("routw", [128, 2, 1024], BF16))
                outw = A(nc.sbuf_tensor("outw", [128, 8, 1024], BF16))
                P_sb = wbuf[:, 2:4, :].rearrange("p a b -> p (a b)")
                P2_sb = wbuf[:, 4:6, :].rearrange("p a b -> p (a b)")
                PT_sb = wbuf[:, 0:2, :].rearrange("p a b -> p (a b)")
                O_sb = wbuf2[:, 0:2, :].rearrange("p a b -> p (a b)")
                R_sb = wbuf2[:, 2:4, :].rearrange("p a b -> p (a b)")
                mix_sb = wbuf[:, 6:8, :].rearrange("p a b -> p (a b)")
                lat_sb = A(nc.sbuf_tensor("lat_sb", [128, 256], BF16))
                latT_sb = A(nc.sbuf_tensor("latT_sb", [128, 256], BF16))
                lsum = A(nc.sbuf_tensor("lsum", [128, 4], F32))
                linv = A(nc.sbuf_tensor("linv", [128, 4], F32))
                gsum = A(nc.sbuf_tensor("gsum", [16, 128], F32))
                sums_v = A(nc.sbuf_tensor("sums_v", [128, 16], F32))
                identf = A(nc.sbuf_tensor("identf_s", [128, 128], F32))
                ginv = A(nc.sbuf_tensor("ginv", [16, 128], F32))
                gg = A(nc.sbuf_tensor("gg", [16, 128], F32))
                gp_bf = A(nc.sbuf_tensor("gp_bf", [16, 128], BF16))
                om_bf = A(nc.sbuf_tensor("om_bf", [16, 128], BF16))

                tokIdf = sc.dma("sync", identf[:, :], IDENTF, "dx")
                sc.wait("sync", castW["out"])
                tokOutw = sc.dma("sync", outw[:, :, :],
                                 out_bf.ap().rearrange("(c p) e -> p c e",
                                                       p=128), "dx")
                sc.wait("sync", castW["rout"])
                tokRoutw = sc.dma("sync", routw[:, :, :],
                                  rout_bf.ap().rearrange("(c p) e -> p c e",
                                                         p=128), "dx")

                ngroups = (NSUB + 3) // 4
                for g in range(ngroups):
                    su0 = 4 * g
                    nsu = min(4, NSUB - su0)
                    gcols = 128 * nsu
                    new_tiles = ([0, 1, 2, 3, 4, 5] if g == 0 else
                                 list(range(4 * g + 2, min(4 * g + 6, NT))))
                    with ExitStack() as PTK:
                        psl8 = [PTK.enter_context(
                            nc.psum_tensor(U("ps_p%d" % j), [128, 512], F32))
                            for j in range(8)]
                        QW = qkv_bf.ap().rearrange("(c p) e -> p c e",
                                                   p=128)
                        loads = []
                        comps = []

                        def mk_qrq(wsrc, ncw, nch, ycol0, n, bias_fn,
                                   dst_fn, wtok=None):
                            loads.append((wsrc, ncw, wtok))

                            def fn(buf, pset):
                                for dch in range(nch):
                                    for c in range(8):
                                        sc.op("tensor", "matmul",
                                              pset[dch][:, 0:n],
                                              buf[:, c,
                                                  128 * dch:128 * dch + 128],
                                              y2_cm[:, c, ycol0:ycol0 + n],
                                              start=(c == 0), stop=False)
                                    sc.op("tensor", "matmul",
                                          pset[dch][:, 0:n], bias_fn(dch),
                                          onesb[:, 0:n],
                                          start=False, stop=True)
                                peF = sc.fence("tensor")
                                sc.wait("vector", peF)
                                for dch in range(nch):
                                    dst_fn(dch, pset)
                                evF = sc.fence("vector")
                                return peF, evF
                            comps.append(fn)

                        for rnd in range(2):
                            mk_qrq(QW[:, :, 512 * rnd:512 * rnd + 512], 512,
                                   4, 128 + 128 * su0, gcols,
                                   lambda dch, rnd=rnd:
                                       qkvb[:, 512 * rnd + 128 * dch:
                                            512 * rnd + 128 * dch + 128],
                                   lambda dch, pset, rnd=rnd: sc.op(
                                       "vector", "tensor_copy",
                                       q_grp[:, 4 * rnd + dch, 0:gcols],
                                       pset[dch][:, 0:gcols]),
                                   wtok=castW["qkv"])
                        runs = []
                        for t in new_tiles:
                            slot = t % 6
                            if runs and runs[-1][-1][1] == slot - 1 \
                                    and len(runs[-1]) < 4:
                                runs[-1].append((t, slot))
                            else:
                                runs.append([(t, slot)])
                        for run in runs:
                            t0, sl0 = run[0]
                            n = 128 * len(run)
                            for rnd in range(2):
                                mk_qrq(QW[:, :, 1024 + 512 * rnd:
                                          1536 + 512 * rnd], 512,
                                       4, 128 * t0, n,
                                       lambda dch, rnd=rnd:
                                           qkvb[:, 1024 + 512 * rnd + 128 * dch:
                                                1152 + 512 * rnd + 128 * dch],
                                       lambda dch, pset, rnd=rnd, sl0=sl0,
                                       n=n: sc.op(
                                           "vector", "tensor_copy",
                                           k_ring[:, 4 * rnd + dch,
                                                  128 * sl0:128 * sl0 + n],
                                           pset[dch][:, 0:n]),
                                       wtok=castW["qkv"])

                        def mk_v(half):
                            loads.append((QW[:, :, 2048 + 512 * half:
                                             2560 + 512 * half], 512,
                                          castW["qkv"]))

                            def fn(buf, pset, half=half):
                                evs = {}
                                peF = None
                                for j, t in enumerate(new_tiles):
                                    slot = t % 6
                                    ps = pset[j % 4]
                                    if j >= 4:
                                        sc.wait("tensor", evs[j - 4])
                                    for c in range(8):
                                        sc.op("tensor", "matmul", ps[:, :],
                                              y2_cm[:, c,
                                                    128 * t:128 * t + 128],
                                              buf[:, c, :],
                                              start=(c == 0), stop=False)
                                    sc.op("tensor", "matmul", ps[:, :],
                                          onescol[:, :],
                                          qkvb[:, 2048 + 512 * half:
                                               2560 + 512 * half],
                                          start=False, stop=True)
                                    peF = sc.fence("tensor")
                                    sc.wait("vector", peF)
                                    sc.op("vector", "tensor_copy",
                                          v_ring[:, slot,
                                                 512 * half:512 * half + 512],
                                          ps[:, :])
                                    evs[j] = sc.fence("vector")
                                return peF, evs[len(new_tiles) - 1]
                            comps.append(fn)

                        if new_tiles:
                            for half in range(2):
                                mk_v(half)
                        mk_qrq(rq_bf.ap().rearrange("(c p) e -> p c e",
                                                    p=128), 256,
                               2, 128 + 128 * su0, gcols,
                               lambda dch: rqb[:, 128 * dch:128 * dch + 128],
                               lambda dch, pset: sc.op(
                                   "vector", "tensor_copy",
                                   rq_grp[:, dch, 0:gcols],
                                   pset[dch][:, 0:gcols]),
                               wtok=castW["rq"])

                        if g == 0:
                            loads.append((rkv_bf.ap().rearrange(
                                "(c p) e -> p c e", p=128), 512,
                                castW["rkv"]))

                            def fn_rkv(buf, pset):
                                sc.wait("tensor", tokRe)
                                sc.wait("tensor", tok_small)
                                prkf, prkl = pset[0], pset[1]
                                for dch2 in range(2):
                                    for c in range(8):
                                        sc.op("tensor", "matmul",
                                              prkf[:, 64 * dch2:64 * dch2 + 64],
                                              buf[:, c,
                                                  128 * dch2:128 * dch2 + 128],
                                              pooled_ab[:, c:512:8],
                                              start=(c == 0), stop=False)
                                    sc.op("tensor", "matmul",
                                          prkf[:, 64 * dch2:64 * dch2 + 64],
                                          rkvb[:, 128 * dch2:128 * dch2 + 128],
                                          onesb[:, 0:64],
                                          start=False, stop=True)
                                tokP = sc.fence("tensor")
                                sc.wait("vector", tokP)
                                sc.op("vector", "tensor_copy", rk_fm[:, :],
                                      prkf[:, 0:128])
                                for c in range(8):
                                    sc.op("tensor", "matmul", prkl[0:64, :],
                                          pooled_ab[:, c:512:8], buf[:, c, :],
                                          start=(c == 0), stop=False)
                                sc.op("tensor", "matmul", prkl[0:64, :],
                                      onescol[:, 0:64], rkvb[:, :],
                                      start=False, stop=True)
                                peF = sc.fence("tensor")
                                sc.wait("vector", peF)
                                sc.op("vector", "tensor_copy", rv_lm[0:64, :],
                                      prkl[0:64, 256:512])
                                sc.op("vector", "tensor_copy", rv_lm[64:128, :],
                                      prkl[0:64, 256:512])
                                evF = sc.fence("vector")
                                return peF, evF
                            comps.append(fn_rkv)

                        bufs = [wbuf, wbuf2]
                        tokL, peFs, evFs = {}, {}, {}

                        def issue(i):
                            if i - 2 in peFs:
                                sc.wait("sync", peFs[i - 2])
                            wsrc, ncw, wtok = loads[i]
                            if wtok is not None:
                                sc.wait("sync", wtok)
                            tokL[i] = sc.dma("sync",
                                             bufs[i % 2][:, :, 0:ncw], wsrc,
                                             "dx")
                        issue(0)
                        if len(loads) > 1:
                            issue(1)
                        if g == 0:
                            sc.wait("gpsimd", tokPool)
                            tokCI = sc.dma("gpsimd",
                                           cc_in.ap().rearrange("(p q) -> p q",
                                                                p=128),
                                           pooled_m[:, :], "dg")
                            sc.wait("gpsimd", tokCI)
                            sc.op("gpsimd", "collective_compute", "AllGather",
                                  ALU.bypass, [list(range(NCORE))],
                                  [cc_in.ap()], [cc_out.ap()])
                            tokCC = sc.fence("gpsimd", "cc")
                            sc.wait("gpsimd", tokCC)
                            tokRe = sc.dma(
                                "gpsimd",
                                pooled_ab[:, :].rearrange("p (r w) -> p r w",
                                                          r=8),
                                cc_out.ap().rearrange("r (p w) -> p r w",
                                                      p=128), "dg")
                            if "pooled" in dbg:
                                sc.wait("gpsimd", tokRe)
                                sc.dma("gpsimd", dbg["pooled"],
                                       pooled_ab[:, :], "dg")
                        for i, fn in enumerate(comps):
                            sc.wait("tensor", tokL[i])
                            if i - 2 in evFs:
                                sc.wait("tensor", evFs[i - 2])
                            peFs[i], evFs[i] = fn(
                                bufs[i % 2],
                                psl8[0:4] if i % 2 == 0 else psl8[4:8])
                            if i + 2 < len(loads):
                                issue(i + 2)
                        sc.wait("tensor", evFs[len(comps) - 1])
                        if len(comps) >= 2:
                            sc.wait("tensor", evFs[len(comps) - 2])
                    sc.barrier()
                    if stop == "prod%d" % g:
                        return early_finish()

                    prevTail = None
                    for si in range(nsu):
                        su = su0 + si
                        qc = 128 * si
                        if ("only%d" % su) not in skip and any(
                                s.startswith("only") for s in skip):
                            continue
                        if prevTail is not None:
                            sc.wait("scalar", prevTail["res"])
                            sc.wait("scalar", prevTail["ao"])
                            sc.wait("vector", prevTail["ao"])
                        with nc.psum_tensor(U("ps_o"), [128, 1024], F32) as ps_o, \
                             nc.psum_tensor(U("ps_sum"), [128, 128], F32) as ps_sum, \
                             nc.psum_tensor(U("ps_sc0"), [128, 1024], F32) as sc0, \
                             nc.psum_tensor(U("ps_sc1"), [128, 1024], F32) as sc1, \
                             nc.psum_tensor(U("ps_pt"), [128, 1024], BF16) as ps_pt:
                            scs = [sc0, sc1]
                            Ps = [P_sb, P2_sb]
                            tokSum = {}
                            tokSc, tokExp, tokPT, tokPTe, tokAV = {}, {}, {}, {}, {}

                            def emit_scores(r):
                                if r >= 2:
                                    sc.wait("tensor", tokExp[r - 2])
                                slot0 = su % 6
                                for hh in range(4):
                                    h = 4 * r + hh
                                    chq = h // 2
                                    bp = 64 * (h % 2)
                                    cb = 256 * hh
                                    if slot0 != 5:
                                        ro2 = (384 if su == 0 else
                                               (0 if su == 1 else 128))
                                        sc.op("tensor", "matmul",
                                              scs[r % 2][:, cb:cb + 256],
                                              q_grp[bp:bp + 64, chq, qc:qc + 128],
                                              k_ring[bp:bp + 64, chq,
                                                     128 * slot0:128 * slot0 + 256],
                                              start=True, stop=False)
                                        sc.op("tensor", "matmul",
                                              scs[r % 2][:, cb:cb + 256],
                                              onescol[:, :],
                                              negall[0:1, ro2:ro2 + 256],
                                              start=False, stop=True)
                                        continue
                                    for ks in range(2):
                                        slot = (su + ks) % 6
                                        cbk = cb + 128 * ks
                                        sc.op("tensor", "matmul",
                                              scs[r % 2][:, cbk:cbk + 128],
                                              q_grp[bp:bp + 64, chq, qc:qc + 128],
                                              k_ring[bp:bp + 64, chq,
                                                     128 * slot:128 * slot + 128],
                                              start=True, stop=False)
                                        sc.op("tensor", "matmul",
                                              scs[r % 2][:, cbk:cbk + 128],
                                              onescol[:, :],
                                              negall[0:1, 128:256],
                                              start=False, stop=True)
                                tokSc[r] = sc.fence("tensor")

                            emit_scores(0)
                            emit_scores(1)
                            for rnd in range(4):
                                sc.wait("scalar", tokSc[rnd])
                                if rnd >= 2:
                                    sc.wait("scalar", tokPT[rnd - 2])
                                    sc.wait("scalar", tokSum[rnd - 2])
                                if "exp" in skip:
                                    sc.op("scalar", "copy", junk[:, 0:1],
                                          epsv[:, :])
                                else:
                                    sc.op("scalar", "activation",
                                          Ps[rnd % 2][:, :],
                                          scs[rnd % 2][:, :], ACT.Exp,
                                          scale=0.125)
                                tokExp[rnd] = sc.fence("scalar")
                                sc.wait("vector", tokExp[rnd])
                                if "tri" in skip or "exp" in skip:
                                    sc.op("vector", "tensor_copy", junk[:, 1:2],
                                          epsv[:, :])
                                else:
                                    sc.op("vector", "tensor_mul",
                                          Ps[rnd % 2][:, :],
                                          Ps[rnd % 2][:, :], tri[:, :])
                                tokTri = sc.fence("vector")
                                sc.wait("tensor", tokTri)
                                for hh in range(4):
                                    for ks in range(2):
                                        cb = 256 * hh + 128 * ks
                                        sc.op("tensor", "transpose",
                                              ps_pt[:, cb:cb + 128],
                                              Ps[rnd % 2][:, cb:cb + 128],
                                              ident[:, :])
                                tokPT[rnd] = sc.fence("tensor")
                                sc.wait("vector", tokPT[rnd])
                                if rnd >= 1:
                                    sc.wait("vector", tokAV[rnd - 1])
                                sc.op("vector", "tensor_copy", PT_sb[:, :],
                                      ps_pt[:, :])
                                tokPTe[rnd] = sc.fence("vector")
                                sc.wait("tensor", tokPTe[rnd])
                                if rnd == 0 and prevTail is not None:
                                    sc.wait("tensor", prevTail["res"])
                                    sc.wait("tensor", prevTail["oev"])
                                for hh in range(4):
                                    h = 4 * rnd + hh
                                    for ks in range(2):
                                        slot = (su + ks) % 6
                                        cb = 256 * hh + 128 * ks
                                        sc.op("tensor", "matmul",
                                              ps_o[64 * (h % 2):64 * (h % 2) + 64,
                                                   128 * (h // 2):128 * (h // 2) + 128],
                                              v_ring[:, slot, 64 * h:64 * h + 64],
                                              PT_sb[:, cb:cb + 128],
                                              start=(ks == 0), stop=(ks == 1))
                                tokAV[rnd] = sc.fence("tensor")
                                for hh in range(4):
                                    h = 4 * rnd + hh
                                    sc.op("vector", "tensor_scalar",
                                          latT_sb[:, 0:256],
                                          Ps[rnd % 2][:, 256 * hh:256 * hh + 256],
                                          1.0, 0.0, ALU.mult, op1=ALU.add,
                                          accum_out=sums_v[:, h:h + 1])
                                tokSum[rnd] = sc.fence("vector")
                                if rnd + 2 < 4:
                                    emit_scores(rnd + 2)
                            vgap()
                            tokSumF = sc.fence("vector")
                            sc.wait("tensor", tokSumF)
                            sc.wait("tensor", tokIdf)
                            sc.op("tensor", "transpose", ps_sum[0:16, :],
                                  sums_v[:, :], identf[:, :])
                            tokTrS = sc.fence("tensor")
                            sc.wait("scalar", tokAV[3])
                            sc.op("scalar", "copy", O_sb[:, :], ps_o[:, :])
                            sc.wait("scalar", tokTrS)
                            sc.op("scalar", "activation", gsum[:, :],
                                  ps_sum[0:16, :], ACT.Identity,
                                  bias=tinyv[0:16, :])
                            tokOev = sc.fence("scalar")
                        sc.wait("vector", tokOev)
                        sc.wait("tensor", tokOev)
                        if stop == "su%d:loc" % su:
                            return early_finish()
                        if "osb" in dbg:
                            sc.wait("gpsimd", tokOev)
                            sc.dma("gpsimd",
                                   dbg["osb"][:, 1024 * su:1024 * su + 1024],
                                   O_sb[:, :], "dg")
                        with nc.psum_tensor(U("ps_lsc"), [128, 256], F32) as ps_lsc:
                            for hp in range(4):
                                bp = 64 * (hp % 2)
                                sc.op("tensor", "matmul",
                                      ps_lsc[:, 64 * hp:64 * hp + 64],
                                      rq_grp[bp:bp + 64, hp // 2, qc:qc + 128],
                                      rk_fm[bp:bp + 64,
                                            64 * (hp // 2):64 * (hp // 2) + 64],
                                      start=True, stop=False)
                                sc.op("tensor", "matmul",
                                      ps_lsc[:, 64 * hp:64 * hp + 64],
                                      onescol[:, :],
                                      latneg[0:1, 64 * (0 if su == 0 else 1 + (su - 1) // 2):64 * (0 if su == 0 else 1 + (su - 1) // 2) + 64],
                                      start=False, stop=True)
                            tokL = sc.fence("tensor")
                            sc.wait("scalar", tokL)
                            for hp in range(4):
                                sc.op("scalar", "activation",
                                      lat_sb[:, 64 * hp:64 * hp + 64],
                                      ps_lsc[:, 64 * hp:64 * hp + 64], ACT.Exp,
                                      scale=0.125, accum_out=lsum[:, hp:hp + 1])
                            sc.op("scalar", "copy", junk[:, 2:3], epsv[:, :])
                            tokL = sc.fence("scalar")
                        sc.wait("vector", tokL)
                        sc.op("vector", "tensor_scalar", lsum[:, :], lsum[:, :],
                              TINY, None, ALU.add)
                        vgap()
                        sc.op("vector", "reciprocal", linv[:, :], lsum[:, :])
                        vgap()
                        tokLi = sc.fence("vector")
                        sc.wait("scalar", tokLi)
                        for hp in range(4):
                            sc.op("scalar", "activation",
                                  lat_sb[:, 64 * hp:64 * hp + 64],
                                  lat_sb[:, 64 * hp:64 * hp + 64], ACT.Identity,
                                  scale=linv[:, hp:hp + 1])
                        tokLn = sc.fence("scalar")
                        sc.wait("tensor", tokLn)
                        with nc.psum_tensor(U("ps_lpt"), [128, 256], BF16) as ps_lpt:
                            for tp in range(2):
                                sc.op("tensor", "transpose",
                                      ps_lpt[:, 128 * tp:128 * tp + 128],
                                      lat_sb[:, 128 * tp:128 * tp + 128],
                                      ident[:, :])
                            tokP = sc.fence("tensor")
                            sc.wait("vector", tokP)
                            sc.op("vector", "tensor_copy", latT_sb[:, :],
                                  ps_lpt[:, :])
                            tokE = sc.fence("vector")
                            sc.wait("tensor", tokE)
                        with nc.psum_tensor(U("ps_lo"), [128, 256], F32) as ps_lo:
                            for hp in range(4):
                                bp = 64 * (hp % 2)
                                sc.op("tensor", "matmul",
                                      ps_lo[bp:bp + 64,
                                            128 * (hp // 2):128 * (hp // 2) + 128],
                                      rv_lm[bp:bp + 64, 64 * hp:64 * hp + 64],
                                      latT_sb[bp:bp + 64,
                                              128 * (hp // 2):128 * (hp // 2) + 128],
                                      start=True, stop=False)
                                sc.op("tensor", "matmul",
                                      ps_lo[bp:bp + 64,
                                            128 * (hp // 2):128 * (hp // 2) + 128],
                                      zrow[0:1, 0:64], onescol[0:1, :],
                                      start=False, stop=True)
                            tokP = sc.fence("tensor")
                            sc.wait("vector", tokP)
                            sc.op("vector", "tensor_copy", lat_sb[:, :],
                                  ps_lo[:, :])
                            tokE = sc.fence("vector")
                            sc.wait("tensor", tokE)
                        if stop == "su%d:lat" % su:
                            return early_finish()
                        with nc.psum_tensor(U("ps_r2"), [128, 1024], F32) as ps_r2:
                            sc.wait("tensor", tokRoutw)
                            for dch in range(8):
                                for a in range(2):
                                    sc.op("tensor", "matmul",
                                          ps_r2[:, 128 * dch:128 * dch + 128],
                                          routw[:, a, 128 * dch:128 * dch + 128],
                                          lat_sb[:, 128 * a:128 * a + 128],
                                          start=(a == 0), stop=False)
                                sc.op("tensor", "matmul",
                                      ps_r2[:, 128 * dch:128 * dch + 128],
                                      routb[:, 128 * dch:128 * dch + 128],
                                      onesb[:, 0:128], start=False, stop=True)
                            tokP = sc.fence("tensor")
                            sc.wait("scalar", tokP)
                            sc.op("scalar", "copy", R_sb[:, :], ps_r2[:, :])
                            tokRe2 = sc.fence("scalar")
                        sc.wait("tensor", tokRe2)
                        if stop == "su%d:r" % su:
                            return early_finish()
                        if "rsb" in dbg:
                            sc.wait("gpsimd", tokRe2)
                            sc.dma("gpsimd",
                                   dbg["rsb"][:, 1024 * su:1024 * su + 1024],
                                   R_sb[:, :], "dg")
                        with nc.psum_tensor(U("ps_g"), [128, 128], F32) as ps_g:
                            for c in range(8):
                                sc.op("tensor", "matmul", ps_g[0:16, :],
                                      glG[:, 16 * c:16 * c + 16],
                                      O_sb[:, 128 * c:128 * c + 128],
                                      start=(c == 0), stop=(c == 7))
                            for c in range(8):
                                sc.op("tensor", "matmul", ps_g[32:48, :],
                                      grG[:, 16 * c:16 * c + 16],
                                      R_sb[:, 128 * c:128 * c + 128],
                                      start=(c == 0), stop=(c == 7))
                            tokP = sc.fence("tensor")
                            sc.wait("vector", tokP)
                            sc.op("vector", "reciprocal", ginv[:, :],
                                  gsum[:, :])
                            vgap()
                            sc.op("vector", "tensor_mul", gg[:, :],
                                  ps_g[0:16, :], ginv[:, :])
                            vgap()
                            sc.op("vector", "tensor_add", gg[:, :], gg[:, :],
                                  ps_g[32:48, :])
                            vgap()
                            tokGv = sc.fence("vector")
                        sc.wait("tensor", tokGv)
                        sc.wait("scalar", tokGv)
                        sc.op("scalar", "activation", gg[:, :], gg[:, :],
                              ACT.Tanh, scale=0.5)
                        tokGs = sc.fence("scalar")
                        sc.wait("vector", tokGs)
                        sc.op("vector", "tensor_scalar", om_bf[:, :], gg[:, :],
                              -0.5, 0.5, ALU.mult, op1=ALU.add)
                        sc.op("vector", "tensor_scalar", gg[:, :], gg[:, :],
                              0.5, 0.5, ALU.mult, op1=ALU.add)
                        vgap()
                        sc.op("vector", "tensor_mul", gp_bf[:, :], gg[:, :],
                              ginv[:, :])
                        tokGb = sc.fence("vector")
                        sc.wait("tensor", tokGb)
                        if stop == "su%d:g" % su:
                            return early_finish()
                        if "gate" in dbg:
                            sc.wait("sync", tokGs)
                            sc.dma("sync", dbg["gate"][:, 128 * su:128 * su + 128],
                                   gg[:, :], "st")
                        with nc.psum_tensor(U("ps_bc"), [128, 1024], F32) as ps_bc:
                            for c in range(8):
                                sc.op("tensor", "matmul",
                                      ps_bc[:, 128 * c:128 * c + 128],
                                      ebc[:, 128 * c:128 * c + 128],
                                      gp_bf[:, :], start=True, stop=False)
                                sc.op("tensor", "matmul",
                                      ps_bc[:, 128 * c:128 * c + 128],
                                      zrow[0:1, :], onescol[0:1, :],
                                      start=False, stop=True)
                            tokP = sc.fence("tensor")
                            sc.wait("vector", tokP)
                            sc.op("vector", "tensor_mul", mix_sb[:, :],
                                  ps_bc[:, :], O_sb[:, :])
                            tokM1 = sc.fence("vector")
                            sc.wait("tensor", tokM1)
                            for c in range(8):
                                sc.op("tensor", "matmul",
                                      ps_bc[:, 128 * c:128 * c + 128],
                                      ebc[:, 128 * c:128 * c + 128],
                                      om_bf[:, :], start=True, stop=False)
                                sc.op("tensor", "matmul",
                                      ps_bc[:, 128 * c:128 * c + 128],
                                      zrow[0:1, :], onescol[0:1, :],
                                      start=False, stop=True)
                            tokP = sc.fence("tensor")
                            sc.wait("vector", tokP)
                            sc.op("vector", "tensor_mul", P_sb[:, :],
                                  ps_bc[:, :], R_sb[:, :])
                            sc.op("vector", "tensor_add", mix_sb[:, :],
                                  mix_sb[:, :], P_sb[:, :])
                            tokMx = sc.fence("vector")
                        sc.wait("tensor", tokMx)
                        with nc.psum_tensor(U("ps_ao"), [128, 1024], F32) as ps_ao:
                            sc.wait("tensor", tokOutw)
                            for half in range(2):
                                for c in range(8):
                                    sc.op("tensor", "matmul",
                                          ps_ao[:, 512 * half:512 * half + 512],
                                          mix_sb[:, 128 * c:128 * c + 128],
                                          outw[:, c, 512 * half:512 * half + 512],
                                          start=(c == 0), stop=False)
                                sc.op("tensor", "matmul",
                                      ps_ao[:, 512 * half:512 * half + 512],
                                      onescol[:, :],
                                      outb[:, 512 * half:512 * half + 512],
                                      start=False, stop=True)
                            tokAoPE = sc.fence("tensor")
                            sc.wait("vector", tokAoPE)
                            sc.op("vector", "tensor_add", x_sb[:, su + 1, :],
                                  x_sb[:, su + 1, :], ps_ao[:, :])
                            tokRes = sc.fence("vector")
                        if si == nsu - 1:
                            sc.wait("tensor", tokRes)
                        if su == 0:
                            sc.wait("scalar", tokRes)
                            sc.op("scalar", "activation", x_sb[:, 1, :],
                                  x_sb[:, 1, :], ACT.Identity,
                                  scale=hkeep[:, :])
                            sc.fence("scalar")
                        if si == nsu - 1:
                            sc.barrier()
                            prevTail = None
                        else:
                            prevTail = {"res": tokRes, "ao": tokAoPE,
                                        "oev": tokOev}
                        if stop == "su%d" % su:
                            return early_finish()
        if "x2" in dbg:
            sc.dma("sync", dbg["x2"].rearrange("(s p) d -> p s d", p=128),
                   x_sb[:, :, :], "st")
            sc.barrier()

        with nc.sbuf_tensor("yw3", [128, 8, 2305], BF16) as yw3:
            sc.op("vector", "memset", yw3[:, :, 0:1], 0.0)
            ln_sweep(list(range(1, NT)), yw3, 1)
            if "y3fm" in dbg:
                sc.dma("gpsimd", dbg["y3fm"].rearrange("p (c t) -> p c t", c=8),
                       yw3[:, :, :], "dg")
                sc.barrier()
            if stop == "ln3":
                return early_finish()
            with nc.sbuf_tensor("wt_sb2", [128, 24, 1024], BF16) as wt_sb2:
                ffn_sweep(yw3, 129, 2048, 2, wt_sb2, store=True)

        sc.wait_queue("sync", "st")

        with nc.Block() as block:
            sc.replay(block)

    return nc


def _bf(a):
    return np.asarray(a, dtype=BF)


def make_consts(core):
    o = {}
    o["ident"] = _bf(np.eye(128, dtype=np.float32))
    o["identf"] = np.eye(128, dtype=np.float32)
    i = np.arange(128)[:, None]
    j = np.arange(256)[None, :]
    tri1 = ((j >= i) & (j < i + 128)).astype(np.float32)
    o["tri"] = _bf(np.tile(tri1, (1, 4)))
    es = np.zeros((128, 256), np.float32)
    for h in range(16):
        es[:, 16 * h + h] = 1.0
    o["esum"] = _bf(es)
    eb = np.zeros((16, 1024), np.float32)
    for c in range(8):
        for p in range(128):
            eb[2 * c + p // 64, 128 * c + p] = 1.0
    o["ebc"] = _bf(eb)
    nr = np.zeros((1, 640), np.float32)
    if core == 0:
        nr[0, 0:128] = NEG
        nr[0, 384:640] = NEG
    o["negall"] = _bf(nr)
    ln = np.zeros((1, 576), np.float32)
    l = np.arange(64)
    for ridx in range(9):
        if ridx == 0:
            m = 8 * core - 1 if core > 0 else 0
        else:
            m = 8 * core + (ridx - 1)
        ln[0, 64 * ridx:64 * ridx + 64] = np.where(l >= m, NEG, 0.0)
    o["latneg"] = _bf(ln)
    o["hkeep"] = np.full((128, 1), 0.0 if core == 0 else 1.0, np.float32)
    return o


def kernel(L0_attn_x, L0_attn_qkv_w, L0_attn_qkv_b, L0_attn_rq_w, L0_attn_rq_b,
           L0_attn_rkv_w, L0_attn_rkv_b, L0_attn_rout_w, L0_attn_rout_b,
           L0_attn_out_w, L0_attn_out_b, L0_attn_gl_w, L0_attn_gr_w,
           L0_ffn_W_r, L0_ffn_W_k, L0_ffn_W_v, L0_ffn_mu_r, L0_ffn_mu_k):
    x = np.ascontiguousarray(np.asarray(L0_attn_x, dtype=np.float32)[0])
    S = x.shape[0]
    xp = np.concatenate([np.zeros((256, D), np.float32), x], axis=0)

    shared = {
        "qkv_w": np.ascontiguousarray(L0_attn_qkv_w, dtype=np.float32),
        "qkv_b": np.asarray(L0_attn_qkv_b, np.float32).reshape(1, 3 * D),
        "rq_w": np.ascontiguousarray(L0_attn_rq_w, dtype=np.float32),
        "rq_b": np.asarray(L0_attn_rq_b, np.float32).reshape(1, 256),
        "rkv_w": np.ascontiguousarray(L0_attn_rkv_w, dtype=np.float32),
        "rkv_b": np.asarray(L0_attn_rkv_b, np.float32).reshape(1, 512),
        "rout_w": np.ascontiguousarray(L0_attn_rout_w, dtype=np.float32),
        "rout_b": np.asarray(L0_attn_rout_b, np.float32).reshape(1, D),
        "out_w": np.ascontiguousarray(L0_attn_out_w, dtype=np.float32),
        "out_b": np.asarray(L0_attn_out_b, np.float32).reshape(1, D),
        "gl_w": np.asarray(L0_attn_gl_w, np.float32).reshape(64, 1),
        "gr_w": np.asarray(L0_attn_gr_w, np.float32).reshape(64, 1),
        "W_r": np.ascontiguousarray(L0_ffn_W_r, dtype=np.float32),
        "W_k": np.ascontiguousarray(L0_ffn_W_k, dtype=np.float32),
        "W_v": np.ascontiguousarray(L0_ffn_W_v, dtype=np.float32),
        "mu_r": np.asarray(L0_ffn_mu_r, np.float32).reshape(8, 128).T.copy(),
        "mu_k": np.asarray(L0_ffn_mu_k, np.float32).reshape(8, 128).T.copy(),
    }
    in_maps = []
    for core in range(NCORE):
        m = dict(shared)
        m["x"] = np.ascontiguousarray(
            xp[2048 * core:2048 * core + NT * 128, :])
        m.update(make_consts(core))
        in_maps.append(m)

    nc = build_program(DEBUG_TAPS)
    res = run_bass_kernel_spmd(nc, in_maps, core_ids=list(range(NCORE)))
    out = np.concatenate([res.results[i]["out"] for i in range(NCORE)], axis=0)
    kernel.last_results = res
    return out.reshape(1, S, D)


# version 17 (final)
# speedup vs baseline: 1.0125x; 1.0125x over previous
import numpy as np
import ml_dtypes
import concourse.bass as bass
import concourse.mybir as mybir
from concourse.bass_utils import run_bass_kernel_spmd
from contextlib import ExitStack

F32 = mybir.dt.float32
BF16 = mybir.dt.bfloat16
ACT = mybir.ActivationFunctionType
ALU = mybir.AluOpType
BF = ml_dtypes.bfloat16

NCORE = 8
TOWN = 2048
NT = 18
NSUB = 17
D = 1024
NEG = -1.0e4
TINY = 1e-12

DEBUG_TAPS = ()


class Sched:
    ENGS = ("tensor", "vector", "scalar", "sync", "gpsimd")
    ENG_SEM = {"tensor": "pe", "vector": "vv", "scalar": "ss",
               "sync": "sy", "gpsimd": "gp"}

    POOLS = {"dx": 8, "dg": 16, "st": 16}

    def __init__(self, nc, stack):
        self.nc = nc
        self.plan = {e: [] for e in self.ENGS}
        names = ["pe", "vv", "ss", "sy", "gp", "cc"]
        for q, np_ in self.POOLS.items():
            names += ["%s%d" % (q, j) for j in range(np_)]
        self.sems = {n: stack.enter_context(nc.semaphore(n)) for n in names}
        self.cnt = {n: 0 for n in names}
        self.rr = {q: 0 for q in self.POOLS}
        self.lastw = {}
        self.marks = {e: 0 for e in self.ENGS}
        self.sem_name = {self.sems[n]: n for n in names}

    def op(self, eng, meth, *args, **kw):
        self.plan[eng].append({"m": meth, "a": args, "k": kw, "i": []})

    def fence(self, eng, name=None):
        items = self.plan[eng]
        for i in range(len(items) - 1, -1, -1):
            it = items[i]
            if it["m"] == "wait_ge":
                continue
            if it["i"]:
                sem, k = it["i"][-1]
                n = self.sem_name[sem]
                return (n, self.cnt[n])
            n = name or self.ENG_SEM[eng]
            k = 16 if it["m"].startswith("dma") else 1
            self.cnt[n] += k
            it["i"].append((self.sems[n], k))
            return (n, self.cnt[n])
        raise RuntimeError(f"no instruction to fence on {eng}")

    def dma(self, eng, out, in_, queue):
        n = "%s%d" % (queue, self.rr[queue] % self.POOLS[queue])
        self.rr[queue] += 1
        self.cnt[n] += 16
        self.plan[eng].append({"m": "dma_start", "a": (),
                               "k": {"out": out, "in_": in_},
                               "i": [(self.sems[n], 16)]})
        return (n, self.cnt[n])

    def wait_queue(self, eng, queue):
        for j in range(self.POOLS[queue]):
            self.wait(eng, ("%s%d" % (queue, j), self.cnt["%s%d" % (queue, j)]))

    def wait(self, eng, tok):
        name, val = tok
        if val <= 0:
            return
        key = (eng, name)
        if self.lastw.get(key, 0) >= val:
            return
        self.lastw[key] = val
        self.plan[eng].append({"m": "wait_ge", "a": (self.sems[name], val),
                               "k": {}, "i": []})

    def barrier(self):
        fences = []
        for eng in self.ENGS:
            if any(it["m"] != "wait_ge" for it in self.plan[eng][self.marks[eng]:]):
                fences.append(self.fence(eng))
        for q, np_ in self.POOLS.items():
            for j in range(np_):
                n = "%s%d" % (q, j)
                if self.cnt[n]:
                    fences.append((n, self.cnt[n]))
        fences.append(("cc", self.cnt["cc"]))
        for eng in self.ENGS:
            for tok in fences:
                self.wait(eng, tok)
        for eng in self.ENGS:
            self.marks[eng] = len(self.plan[eng])

    def wait_all(self, eng, toks):
        for tok in toks:
            self.wait(eng, tok)

    def replay(self, block):
        def mk(eng):
            def run(e):
                for it in self.plan[eng]:
                    r = getattr(e, it["m"])(*it["a"], **it["k"])
                    for sem, k in it["i"]:
                        r.then_inc(sem, k)
            return run
        for eng in self.ENGS:
            getattr(block, eng)(mk(eng))


def build_program(debug_taps=(), stop=None, skip=frozenset()):
    nc = bass.Bass("TRN2", target_bir_lowering=False, detect_race_conditions=False)

    def inp(name, shape, dt=F32):
        return nc.dram_tensor(name, shape, dt, kind="ExternalInput").ap()

    X = inp("x", [NT * 128, D])
    QKV_W = inp("qkv_w", [D, 3 * D])
    QKV_B = inp("qkv_b", [1, 3 * D])
    RQ_W = inp("rq_w", [D, 256])
    RQ_B = inp("rq_b", [1, 256])
    RKV_W = inp("rkv_w", [D, 512])
    RKV_B = inp("rkv_b", [1, 512])
    ROUT_W = inp("rout_w", [256, D])
    ROUT_B = inp("rout_b", [1, D])
    OUT_W = inp("out_w", [D, D])
    OUT_B = inp("out_b", [1, D])
    GL_W = inp("gl_w", [64, 1])
    GR_W = inp("gr_w", [64, 1])
    W_R = inp("W_r", [D, D])
    W_K = inp("W_k", [D, D])
    W_V = inp("W_v", [D, D])
    MU_R = inp("mu_r", [128, 8])
    MU_K = inp("mu_k", [128, 8])
    IDENT = inp("ident", [128, 128], BF16)
    IDENTF = inp("identf", [128, 128], F32)
    TRI = inp("tri", [128, 1024], BF16)
    ESUM = inp("esum", [128, 256], BF16)
    EBC = inp("ebc", [16, 1024], BF16)
    NEGALL = inp("negall", [1, 640], BF16)
    LATNEG = inp("latneg", [1, 576], BF16)
    HKEEP = inp("hkeep", [128, 1], F32)

    OUT = nc.dram_tensor("out", [TOWN, D], F32, kind="ExternalOutput").ap()

    wT_d = [nc.dram_tensor(n, [D, D], BF16, kind="Internal")
            for n in ("wrT_d", "wkT_d", "wvT_d")]
    qkv_bf = nc.dram_tensor("qkv_bf", [D, 3 * D], BF16, kind="Internal")
    rq_bf = nc.dram_tensor("rq_bf", [D, 256], BF16, kind="Internal")
    rkv_bf = nc.dram_tensor("rkv_bf", [D, 512], BF16, kind="Internal")
    rout_bf = nc.dram_tensor("rout_bf", [256, D], BF16, kind="Internal")
    out_bf = nc.dram_tensor("out_bf", [D, D], BF16, kind="Internal")
    cc_in = nc.dram_tensor("cc_in", [8192], F32, kind="Internal")
    cc_out = nc.dram_tensor("cc_out", [8, 8192], F32, kind="Internal")

    dbg_shapes = {"x1": [NT * 128, D], "x2": [NT * 128, D],
                  "y1fm": [128, 8 * 2305], "lnstat": [128, 5 * NT],
                  "y2fm": [128, 8 * 2304], "pooled": [128, 512],
                  "y3fm": [128, 8 * 2305],
                  "osb": [128, NSUB * 1024], "rsb": [128, NSUB * 1024],
                  "gate": [16, NSUB * 128]}
    dbg = {t: nc.dram_tensor("dbg_" + t, dbg_shapes[t], F32,
                             kind="ExternalOutput").ap()
           for t in debug_taps}

    _uc = [0]

    def U(n):
        _uc[0] += 1
        return "%s_u%d" % (n, _uc[0])

    with ExitStack() as STK:
        E = STK.enter_context
        sc = Sched(nc, STK)

        x_sb = E(nc.sbuf_tensor("x_sb", [128, NT, D], F32))
        ident = E(nc.sbuf_tensor("ident_s", [128, 128], BF16))
        mur = E(nc.sbuf_tensor("mur_s", [128, 8], F32))
        muk = E(nc.sbuf_tensor("muk_s", [128, 8], F32))
        onesb = E(nc.sbuf_tensor("onesb", [1, 1024], BF16))
        onescol = E(nc.sbuf_tensor("onescol", [1, 128], BF16))
        zrow = E(nc.sbuf_tensor("zrow", [1, 128], BF16))
        epsv = E(nc.sbuf_tensor("epsv", [128, 1], F32))
        junk = E(nc.sbuf_tensor("junk", [128, 132], F32))
        tinyv = E(nc.sbuf_tensor("tinyv", [128, 1], F32))
        ln_sums = E(nc.sbuf_tensor("ln_sums", [128, NT], F32))
        ln_sq = E(nc.sbuf_tensor("ln_sq", [128, NT], F32))
        ln_var = E(nc.sbuf_tensor("ln_var", [128, NT], F32))
        ln_inv = E(nc.sbuf_tensor("ln_inv", [128, NT], F32))
        ln_nmi = E(nc.sbuf_tensor("ln_nmi", [128, NT], F32))

        def vgap():
            sc.op("vector", "tensor_copy", junk[:, 64:128], junk[:, 0:64])

        tok_x = sc.dma("sync", x_sb[:, :, :],
                       X.rearrange("(s p) d -> p s d", p=128), "dx")
        cons_toks = {}
        for dst, srcx in ((ident, IDENT), (mur, MU_R), (muk, MU_K)):
            cons_toks[dst.name] = sc.dma("sync", dst[:, :], srcx, "dx")
        sc.op("vector", "memset", junk[:, :], 0.0)
        sc.op("vector", "memset", onesb[:, :], 1.0)
        sc.op("vector", "memset", onescol[:, :], 1.0)
        sc.op("vector", "memset", zrow[:, :], 0.0)
        sc.op("vector", "memset", epsv[:, :], 1e-5)
        sc.op("vector", "memset", tinyv[:, :], TINY)


        def ln_sweep(tiles, dst, col0):
            sl, sr = tiles[0], tiles[-1] + 1
            with nc.sbuf_tensor("lns_%d_%d" % (sl, col0), [128, 2, D], F32) as scr, \
                 nc.sbuf_tensor("lnv_%d_%d" % (sl, col0), [128, 2, D], F32) as scr2, \
                 nc.sbuf_tensor("lny_%d_%d" % (sl, col0), [128, 2, D], BF16) as yscr:
                for i, s_ in enumerate(tiles):
                    b = i % 2
                    sc.op("scalar", "activation", scr[:, b, :], x_sb[:, s_, :],
                          ACT.Square, accum_out=ln_sq[:, s_:s_ + 1])
                    sc.op("vector", "tensor_scalar", scr2[:, b, :],
                          x_sb[:, s_, :], 1.0, 0.0, ALU.mult, op1=ALU.add,
                          accum_out=ln_sums[:, s_:s_ + 1])
                sc.op("scalar", "copy", junk[:, 0:1], epsv[:, :])
                tokS = sc.fence("scalar")
                sc.wait("vector", tokS)
                vgap()
                sc.op("vector", "tensor_scalar", ln_sums[:, sl:sr],
                      ln_sums[:, sl:sr], 1.0 / D, None, ALU.mult)
                sc.op("vector", "tensor_scalar", ln_sq[:, sl:sr],
                      ln_sq[:, sl:sr], 1.0 / D, None, ALU.mult)
                sc.op("vector", "scalar_tensor_tensor", ln_var[:, sl:sr],
                      ln_sums[:, sl:sr], -1.0, ln_sums[:, sl:sr],
                      op0=ALU.mult, op1=ALU.mult)
                vgap()
                sc.op("vector", "tensor_add", ln_var[:, sl:sr],
                      ln_var[:, sl:sr], ln_sq[:, sl:sr])
                vgap()
                sc.op("vector", "tensor_scalar", ln_var[:, sl:sr],
                      ln_var[:, sl:sr], 1e-5, None, ALU.add)
                tokV = sc.fence("vector")
                sc.wait("scalar", tokV)
                sc.op("scalar", "activation", ln_nmi[:, sl:sr],
                      ln_var[:, sl:sr], ACT.Sqrt)
                tokS = sc.fence("scalar")
                sc.wait("vector", tokS)
                sc.op("vector", "reciprocal", ln_inv[:, sl:sr],
                      ln_nmi[:, sl:sr])
                vgap()
                sc.op("vector", "tensor_mul", ln_nmi[:, sl:sr],
                      ln_inv[:, sl:sr], ln_inv[:, sl:sr])
                vgap()
                sc.op("vector", "tensor_mul", ln_nmi[:, sl:sr],
                      ln_nmi[:, sl:sr], ln_var[:, sl:sr])
                vgap()
                sc.op("vector", "tensor_scalar", ln_nmi[:, sl:sr],
                      ln_nmi[:, sl:sr], -0.5, 1.5, ALU.mult, op1=ALU.add)
                vgap()
                sc.op("vector", "tensor_mul", ln_inv[:, sl:sr],
                      ln_inv[:, sl:sr], ln_nmi[:, sl:sr])
                vgap()
                sc.op("vector", "scalar_tensor_tensor", ln_nmi[:, sl:sr],
                      ln_sums[:, sl:sr], -1.0, ln_inv[:, sl:sr],
                      op0=ALU.mult, op1=ALU.mult)
                tokV = sc.fence("vector")
                sc.wait("scalar", tokV)
                ev = {}
                with nc.psum_tensor(U("ps_ln0"), [128, 1024], BF16) as p0, \
                     nc.psum_tensor(U("ps_ln1"), [128, 1024], BF16) as p1:
                    pls = [p0, p1]
                    for i, s_ in enumerate(tiles):
                        b = i % 2
                        if i >= 2:
                            sc.wait("scalar", ev[i - 2])
                        sc.op("scalar", "activation", yscr[:, b, :],
                              x_sb[:, s_, :], ACT.Identity,
                              bias=ln_nmi[:, s_:s_ + 1], scale=ln_inv[:, s_:s_ + 1])
                        tokS = sc.fence("scalar")
                        sc.wait("tensor", tokS)
                        if i >= 2:
                            sc.wait("tensor", ev[i - 2])
                        for c in range(8):
                            sc.op("tensor", "transpose",
                                  pls[b][:, 128 * c:128 * c + 128],
                                  yscr[:, b, 128 * c:128 * c + 128], ident[:, :])
                        tokP = sc.fence("tensor")
                        sc.wait("vector", tokP)
                        col = col0 + 128 * i
                        sc.op("vector", "tensor_copy", dst[:, :, col:col + 128],
                              pls[b][:, :].rearrange("p (c q) -> p c q", c=8))
                        ev[i] = sc.fence("vector")
            sc.barrier()

        def ffn_sweep(yw, col0, ncols, tile0, wt_sb, store=False,
                      tokWs_pre=None):
            if tokWs_pre is not None:
                tokWs = tokWs_pre
            else:
                tokWs = []
                for wi in range(3):
                    tokWs.append(sc.dma("sync", wt_sb[:, 8 * wi:8 * wi + 8, :],
                                 wT_d[wi].ap().rearrange("(c p) d -> p c d", p=128),
                                 "dx"))
            with nc.sbuf_tensor("fxr%d" % col0, [128, 8, 512], BF16) as xr_b, \
                 nc.sbuf_tensor("fxk%d" % col0, [128, 8, 512], BF16) as xk_b, \
                 nc.sbuf_tensor("fka%d" % col0, [128, 8, 512], BF16) as kact_b, \
                 nc.sbuf_tensor("fkr%d" % col0, [128, 512], F32) as krelu, \
                 nc.sbuf_tensor("frs%d" % col0, [128, D], F32) as r_scr, \
                 nc.sbuf_tensor("fr2%d" % col0, [128, D], F32) as r_scr2:
                nblk = (ncols + 511) // 512
                subs_done = set()

                def emit_subs(b_):
                    c0b = col0 + 512 * b_
                    wb = min(512, ncols - 512 * b_)
                    for c in range(8):
                        sc.op("vector", "tensor_sub", xk_b[:, c, 0:wb],
                              yw[:, c, c0b:c0b + wb],
                              yw[:, c, c0b - 1:c0b - 1 + wb])
                for blk in range(nblk):
                    c0 = col0 + 512 * blk
                    w = min(512, ncols - 512 * blk)
                    sc.wait_all("vector", tokWs)
                    if blk not in subs_done:
                        emit_subs(blk)
                    mixF = {}
                    for c in range(8):
                        sc.op("vector", "scalar_tensor_tensor", xr_b[:, c, 0:w],
                              xk_b[:, c, 0:w], mur[:, c:c + 1],
                              yw[:, c, c0 - 1:c0 - 1 + w],
                              op0=ALU.mult, op1=ALU.add)
                        sc.op("vector", "scalar_tensor_tensor", xk_b[:, c, 0:w],
                              xk_b[:, c, 0:w], muk[:, c:c + 1],
                              yw[:, c, c0 - 1:c0 - 1 + w],
                              op0=ALU.mult, op1=ALU.add)
                        mixF[c] = sc.fence("vector")
                    with nc.psum_tensor(U("ps_k0"), [128, 512], F32) as k0, \
                         nc.psum_tensor(U("ps_k1"), [128, 512], F32) as k1:
                        pks = [k0, k1]
                        kq = {}
                        for dch in range(8):
                            b = dch % 2
                            if dch >= 2:
                                sc.wait("tensor", kq[dch - 2])
                            for cch in range(8):
                                if dch == 0:
                                    sc.wait("tensor", mixF[cch])
                                sc.op("tensor", "matmul", pks[b][:, 0:w],
                                      wt_sb[:, 8 + cch, 128 * dch:128 * dch + 128],
                                      xk_b[:, cch, 0:w],
                                      start=(cch == 0), stop=(cch == 7))
                            tokP = sc.fence("tensor")
                            sc.wait("scalar", tokP)
                            sc.op("scalar", "activation", krelu[:, 0:w],
                                  pks[b][:, 0:w], ACT.Relu)
                            sc.op("scalar", "activation", kact_b[:, dch, 0:w],
                                  krelu[:, 0:w], ACT.Square)
                            kq[dch] = sc.fence("scalar")
                        tok_kact = kq[7]
                    if blk + 1 < nblk:
                        sc.wait("vector", kq[7])
                        emit_subs(blk + 1)
                        subs_done.add(blk + 1)
                    ntile = (w + 127) // 128
                    with nc.psum_tensor(U("ps_r0"), [128, 1024], F32) as pr0, \
                         nc.psum_tensor(U("ps_r1"), [128, 1024], F32) as pr1, \
                         nc.psum_tensor(U("ps_v0"), [128, 1024], F32) as pv0, \
                         nc.psum_tensor(U("ps_v1"), [128, 1024], F32) as pv1:
                        prs = [pr0, pr1]
                        pvs = [pv0, pv1]
                        rsb = [r_scr, r_scr2]
                        hist = {}
                        for ti in range(ntile):
                            b2 = ti % 2
                            pr, pv = prs[b2], pvs[b2]
                            rsc = rsb[b2]
                            tw = min(128, w - 128 * ti)
                            s_out = tile0 + 4 * blk + ti
                            sc.wait("tensor", tok_kact)
                            if ti >= 2:
                                sc.wait("tensor", hist[ti - 2][0])
                                sc.wait("tensor", hist[ti - 2][1])
                            for half in range(2):
                                for cch in range(8):
                                    sc.op("tensor", "matmul",
                                          pr[0:tw, 512 * half:512 * half + 512],
                                          xr_b[:, cch, 128 * ti:128 * ti + tw],
                                          wt_sb[:, cch, 512 * half:512 * half + 512],
                                          start=(cch == 0), stop=(cch == 7))
                            tokR = sc.fence("tensor")
                            for half in range(2):
                                for cch in range(8):
                                    sc.op("tensor", "matmul",
                                          pv[0:tw, 512 * half:512 * half + 512],
                                          kact_b[:, cch, 128 * ti:128 * ti + tw],
                                          wt_sb[:, 16 + cch, 512 * half:512 * half + 512],
                                          start=(cch == 0), stop=(cch == 7))
                            tokV2 = sc.fence("tensor")
                            sc.wait("scalar", tokR)
                            sc.op("scalar", "activation", rsc[0:tw, :],
                                  pr[0:tw, :], ACT.Sigmoid)
                            tokS = sc.fence("scalar")
                            sc.wait("vector", tokV2)
                            sc.wait("vector", tokS)
                            sc.op("vector", "tensor_mul", rsc[0:tw, :],
                                  rsc[0:tw, :], pv[0:tw, :])
                            sc.op("vector", "tensor_add", x_sb[0:tw, s_out, :],
                                  x_sb[0:tw, s_out, :], rsc[0:tw, :])
                            tokRes = sc.fence("vector")
                            hist[ti] = (tokS, tokRes)
                            if store:
                                sc.wait("sync", tokRes)
                                sc.dma("sync",
                                       OUT[128 * (s_out - 2):128 * (s_out - 2) + tw, :],
                                       x_sb[0:tw, s_out, :], "st")
                        sc.wait("tensor", hist[ntile - 1][0])
                        sc.wait("tensor", hist[ntile - 1][1])
                        if ntile >= 2:
                            sc.wait("tensor", hist[ntile - 2][0])
                            sc.wait("tensor", hist[ntile - 2][1])
            sc.barrier()

        def early_finish():
            sc.barrier()
            for s_out in range(2, NT):
                sc.dma("sync", OUT[128 * (s_out - 2):128 * (s_out - 2) + 128, :],
                       x_sb[:, s_out, :], "st")
            sc.wait_queue("sync", "st")
            with nc.Block() as block:
                sc.replay(block)
            return nc

        with nc.sbuf_tensor("yw1", [128, 8, 2305], BF16) as yw1, \
             nc.sbuf_tensor("wt_sb1", [128, 24, 1024], BF16) as wt_sb1, \
             nc.sbuf_tensor("wload", [128, 2, D], BF16) as wload:
            with nc.psum_tensor(U("ps_wt0"), [128, 1024], BF16) as pw0, \
                 nc.psum_tensor(U("ps_wt1"), [128, 1024], BF16) as pw1:
                pwt = [pw0, pw1]
                ev_hist, pe_hist = {}, {}
                for wi, (W, WT) in enumerate(((W_R, wT_d[0]), (W_K, wT_d[1]),
                                              (W_V, wT_d[2]))):
                    for dch in range(8):
                        i = wi * 8 + dch
                        b = i % 2
                        if i >= 2:
                            sc.wait("gpsimd", pe_hist[i - 2])
                        tokL = sc.dma("gpsimd", wload[:, b, :],
                                      W[128 * dch:128 * dch + 128, :], "dg")
                        sc.wait("tensor", tokL)
                        sc.wait("tensor", cons_toks["ident_s"])
                        if i >= 2:
                            sc.wait("tensor", ev_hist[i - 2])
                        for cch in range(8):
                            sc.op("tensor", "transpose",
                                  pwt[b][:, 128 * cch:128 * cch + 128],
                                  wload[:, b, 128 * cch:128 * cch + 128],
                                  ident[:, :])
                        pe_hist[i] = sc.fence("tensor")
                        sc.wait("vector", pe_hist[i])
                        wdst = wt_sb1[:, 8 * wi:8 * wi + 8,
                                      128 * dch:128 * dch + 128]
                        if wi == 2:
                            sc.op("vector", "tensor_scalar", wdst,
                                  pwt[b][:, :].rearrange("p (c q) -> p c q",
                                                         c=8),
                                  0.5, None, ALU.mult)
                        else:
                            sc.op("vector", "tensor_copy", wdst,
                                  pwt[b][:, :].rearrange("p (c q) -> p c q",
                                                         c=8))
                        ev_hist[i] = sc.fence("vector")
                        sc.wait("sync", ev_hist[i])
                        sc.dma(
                            "sync",
                            WT.ap().rearrange("(c p) d -> p c d",
                                              p=128)[:, :, 128 * dch:128 * dch + 128],
                            wdst, "st")
                evlast = ev_hist[23]
                castW = {}
                for nm, bf, fp in (("qkv", qkv_bf, QKV_W),
                                   ("rq", rq_bf, RQ_W),
                                   ("rkv", rkv_bf, RKV_W),
                                   ("rout", rout_bf, ROUT_W),
                                   ("out", out_bf, OUT_W)):
                    castW[nm] = sc.dma("gpsimd", bf.ap(), fp, "dg")
            if stop == "wt":
                return early_finish()
            sc.wait_queue("scalar", "dx")
            sc.wait_queue("vector", "dx")
            sc.op("vector", "memset", yw1[:, :, 0:1], 0.0)
            ln_sweep(list(range(NT)), yw1, 1)
            if "lnstat" in dbg:
                for i2, tns in enumerate((ln_sums, ln_sq, ln_var, ln_inv,
                                          ln_nmi)):
                    sc.dma("sync", dbg["lnstat"][:, NT * i2:NT * i2 + NT],
                           tns[:, :], "st")
                sc.barrier()
            if "y1fm" in dbg:
                sc.dma("gpsimd", dbg["y1fm"].rearrange("p (c t) -> p c t", c=8),
                       yw1[:, :, :], "dg")
                sc.barrier()
            if stop == "ln1":
                return early_finish()
            ffn_sweep(yw1, 1, 2304, 0, wt_sb1, tokWs_pre=[evlast])
        if "x1" in dbg:
            sc.dma("sync", dbg["x1"].rearrange("(s p) d -> p s d", p=128),
                   x_sb[:, :, :], "st")
            sc.barrier()
        if stop == "ffn1":
            return early_finish()

        with ExitStack() as YTK:
            Y = YTK.enter_context
            y2_cm = Y(nc.sbuf_tensor("y2", [128, 8, 2304], BF16))
            tri = Y(nc.sbuf_tensor("tri_s", [128, 1024], BF16))
            esum = Y(nc.sbuf_tensor("esum_s", [128, 256], BF16))
            ebc = Y(nc.sbuf_tensor("ebc_s", [16, 1024], BF16))
            negall = Y(nc.sbuf_tensor("negall_s", [1, 640], BF16))
            latneg = Y(nc.sbuf_tensor("latneg_s", [1, 576], BF16))
            hkeep = Y(nc.sbuf_tensor("hkeep_s", [128, 1], F32))
            qkvb = Y(nc.sbuf_tensor("qkvb", [1, 3 * D], BF16))
            rqb = Y(nc.sbuf_tensor("rqb", [1, 256], BF16))
            rkvb = Y(nc.sbuf_tensor("rkvb", [1, 512], BF16))
            routb = Y(nc.sbuf_tensor("routb", [1, D], BF16))
            outb = Y(nc.sbuf_tensor("outb", [1, D], BF16))
            glG = Y(nc.sbuf_tensor("glG", [128, 128], BF16))
            grG = Y(nc.sbuf_tensor("grG", [128, 128], BF16))
            glv = Y(nc.sbuf_tensor("glv", [128, 1], BF16))
            grv = Y(nc.sbuf_tensor("grv", [128, 1], BF16))
            pooled_m = Y(nc.sbuf_tensor("pooled_m", [128, 64], F32))
            pooled_ab = Y(nc.sbuf_tensor("pooled_ab", [128, 512], BF16))
            rk_fm = Y(nc.sbuf_tensor("rk_fm", [128, 128], BF16))
            rv_lm = Y(nc.sbuf_tensor("rv_lm", [128, 256], BF16))
            for dst, srcx in ((tri, TRI), (esum, ESUM), (ebc, EBC),
                              (negall, NEGALL), (latneg, LATNEG),
                              (hkeep, HKEEP)):
                sc.dma("sync", dst[:, :], srcx, "dx")
            for dst, srcx in ((qkvb, QKV_B), (rqb, RQ_B), (rkvb, RKV_B),
                              (routb, ROUT_B), (outb, OUT_B)):
                sc.dma("gpsimd", dst[:, :], srcx, "dg")
            sc.dma("gpsimd", glv[0:64, :], GL_W, "dg")
            sc.dma("gpsimd", glv[64:128, :], GL_W, "dg")
            sc.dma("gpsimd", grv[0:64, :], GR_W, "dg")
            tok_small = sc.dma("gpsimd", grv[64:128, :], GR_W, "dg")
            sc.op("vector", "memset", glG[:, :], 0.0)
            sc.op("vector", "memset", grG[:, :], 0.0)
            sc.wait_queue("vector", "dg")
            for c in range(8):
                for G, gv in ((glG, glv), (grG, grv)):
                    sc.op("vector", "tensor_copy",
                          G[0:64, 16 * c + 2 * c:16 * c + 2 * c + 1],
                          gv[0:64, :])
                    sc.op("vector", "tensor_copy",
                          G[64:128, 16 * c + 2 * c + 1:16 * c + 2 * c + 2],
                          gv[64:128, :])
            ln_sweep(list(range(NT)), y2_cm, 0)
            if "y2fm" in dbg:
                sc.dma("gpsimd", dbg["y2fm"].rearrange("p (c t) -> p c t", c=8),
                       y2_cm[:, :, :], "dg")
                sc.barrier()

            with nc.sbuf_tensor("pool_scr", [128, 256], BF16) as pool_scr:
                for j in range(8):
                    for c in range(8):
                        sc.op("vector", "tensor_scalar", pool_scr[:, :],
                              y2_cm[:, c, 256 + 256 * j:512 + 256 * j],
                              1.0 / 256.0, 0.0, ALU.mult, op1=ALU.add,
                              accum_out=pooled_m[:, 8 * j + c:8 * j + c + 1])
                vgap()
                sc.op("vector", "tensor_copy", junk[:, 1:2], epsv[:, :])
                tokPool = sc.fence("vector")
            tokRe = None
            if stop == "pooled":
                return early_finish()

            with ExitStack() as ATK:
                A = ATK.enter_context
                q_grp = A(nc.sbuf_tensor("q_grp", [128, 8, 512], BF16))
                rq_grp = A(nc.sbuf_tensor("rq_grp", [128, 2, 512], BF16))
                k_ring = A(nc.sbuf_tensor("k_ring", [128, 8, 768], BF16))
                v_ring = A(nc.sbuf_tensor("v_ring", [128, 6, 1024], BF16))
                wbuf = A(nc.sbuf_tensor("wbuf", [128, 8, 512], BF16))
                wbuf2 = A(nc.sbuf_tensor("wbuf2", [128, 8, 512], BF16))
                routw = A(nc.sbuf_tensor("routw", [128, 2, 1024], BF16))
                outw = A(nc.sbuf_tensor("outw", [128, 8, 1024], BF16))
                P_sb = wbuf[:, 2:4, :].rearrange("p a b -> p (a b)")
                P2_sb = wbuf[:, 4:6, :].rearrange("p a b -> p (a b)")
                PT_sb = wbuf[:, 0:2, :].rearrange("p a b -> p (a b)")
                O_sb = wbuf2[:, 0:2, :].rearrange("p a b -> p (a b)")
                R_sb = wbuf2[:, 2:4, :].rearrange("p a b -> p (a b)")
                mix_sb = wbuf[:, 6:8, :].rearrange("p a b -> p (a b)")
                lat_sb = A(nc.sbuf_tensor("lat_sb", [128, 256], BF16))
                latT_sb = A(nc.sbuf_tensor("latT_sb", [128, 256], BF16))
                lsum = A(nc.sbuf_tensor("lsum", [128, 4], F32))
                linv = A(nc.sbuf_tensor("linv", [128, 4], F32))
                gsum = A(nc.sbuf_tensor("gsum", [16, 128], F32))
                sums_v = A(nc.sbuf_tensor("sums_v", [128, 16], F32))
                identf = A(nc.sbuf_tensor("identf_s", [128, 128], F32))
                ginv = A(nc.sbuf_tensor("ginv", [16, 128], F32))
                gg = A(nc.sbuf_tensor("gg", [16, 128], F32))
                gp_bf = A(nc.sbuf_tensor("gp_bf", [16, 128], BF16))
                om_bf = A(nc.sbuf_tensor("om_bf", [16, 128], BF16))

                tokIdf = sc.dma("sync", identf[:, :], IDENTF, "dx")
                sc.wait("sync", castW["out"])
                tokOutw = sc.dma("sync", outw[:, :, :],
                                 out_bf.ap().rearrange("(c p) e -> p c e",
                                                       p=128), "dx")
                sc.wait("sync", castW["rout"])
                tokRoutw = sc.dma("sync", routw[:, :, :],
                                  rout_bf.ap().rearrange("(c p) e -> p c e",
                                                         p=128), "dx")

                ngroups = (NSUB + 3) // 4
                for g in range(ngroups):
                    su0 = 4 * g
                    nsu = min(4, NSUB - su0)
                    gcols = 128 * nsu
                    new_tiles = ([0, 1, 2, 3, 4, 5] if g == 0 else
                                 list(range(4 * g + 2, min(4 * g + 6, NT))))
                    with ExitStack() as PTK:
                        psl8 = [PTK.enter_context(
                            nc.psum_tensor(U("ps_p%d" % j), [128, 512], F32))
                            for j in range(8)]
                        QW = qkv_bf.ap().rearrange("(c p) e -> p c e",
                                                   p=128)
                        loads = []
                        comps = []

                        def mk_qrq(wsrc, ncw, nch, ycol0, n, bias_fn,
                                   dst_fn, wtok=None):
                            loads.append((wsrc, ncw, wtok))

                            def fn(buf, pset):
                                for dch in range(nch):
                                    for c in range(8):
                                        sc.op("tensor", "matmul",
                                              pset[dch][:, 0:n],
                                              buf[:, c,
                                                  128 * dch:128 * dch + 128],
                                              y2_cm[:, c, ycol0:ycol0 + n],
                                              start=(c == 0), stop=False)
                                    sc.op("tensor", "matmul",
                                          pset[dch][:, 0:n], bias_fn(dch),
                                          onesb[:, 0:n],
                                          start=False, stop=True)
                                peF = sc.fence("tensor")
                                sc.wait("vector", peF)
                                for dch in range(nch):
                                    dst_fn(dch, pset)
                                evF = sc.fence("vector")
                                return peF, evF
                            comps.append(fn)

                        for rnd in range(2):
                            mk_qrq(QW[:, :, 512 * rnd:512 * rnd + 512], 512,
                                   4, 128 + 128 * su0, gcols,
                                   lambda dch, rnd=rnd:
                                       qkvb[:, 512 * rnd + 128 * dch:
                                            512 * rnd + 128 * dch + 128],
                                   lambda dch, pset, rnd=rnd: sc.op(
                                       "vector", "tensor_copy",
                                       q_grp[:, 4 * rnd + dch, 0:gcols],
                                       pset[dch][:, 0:gcols]),
                                   wtok=castW["qkv"])
                        runs = []
                        for t in new_tiles:
                            slot = t % 6
                            if runs and runs[-1][-1][1] == slot - 1 \
                                    and len(runs[-1]) < 4:
                                runs[-1].append((t, slot))
                            else:
                                runs.append([(t, slot)])
                        for run in runs:
                            t0, sl0 = run[0]
                            n = 128 * len(run)
                            for rnd in range(2):
                                mk_qrq(QW[:, :, 1024 + 512 * rnd:
                                          1536 + 512 * rnd], 512,
                                       4, 128 * t0, n,
                                       lambda dch, rnd=rnd:
                                           qkvb[:, 1024 + 512 * rnd + 128 * dch:
                                                1152 + 512 * rnd + 128 * dch],
                                       lambda dch, pset, rnd=rnd, sl0=sl0,
                                       n=n: sc.op(
                                           "vector", "tensor_copy",
                                           k_ring[:, 4 * rnd + dch,
                                                  128 * sl0:128 * sl0 + n],
                                           pset[dch][:, 0:n]),
                                       wtok=castW["qkv"])

                        def mk_v(half):
                            loads.append((QW[:, :, 2048 + 512 * half:
                                             2560 + 512 * half], 512,
                                          castW["qkv"]))

                            def fn(buf, pset, half=half):
                                evs = {}
                                peF = None
                                for j, t in enumerate(new_tiles):
                                    slot = t % 6
                                    ps = pset[j % 4]
                                    if j >= 4:
                                        sc.wait("tensor", evs[j - 4])
                                    for c in range(8):
                                        sc.op("tensor", "matmul", ps[:, :],
                                              y2_cm[:, c,
                                                    128 * t:128 * t + 128],
                                              buf[:, c, :],
                                              start=(c == 0), stop=False)
                                    sc.op("tensor", "matmul", ps[:, :],
                                          onescol[:, :],
                                          qkvb[:, 2048 + 512 * half:
                                               2560 + 512 * half],
                                          start=False, stop=True)
                                    peF = sc.fence("tensor")
                                    sc.wait("vector", peF)
                                    sc.op("vector", "tensor_copy",
                                          v_ring[:, slot,
                                                 512 * half:512 * half + 512],
                                          ps[:, :])
                                    evs[j] = sc.fence("vector")
                                return peF, evs[len(new_tiles) - 1]
                            comps.append(fn)

                        if new_tiles:
                            for half in range(2):
                                mk_v(half)
                        mk_qrq(rq_bf.ap().rearrange("(c p) e -> p c e",
                                                    p=128), 256,
                               2, 128 + 128 * su0, gcols,
                               lambda dch: rqb[:, 128 * dch:128 * dch + 128],
                               lambda dch, pset: sc.op(
                                   "vector", "tensor_copy",
                                   rq_grp[:, dch, 0:gcols],
                                   pset[dch][:, 0:gcols]),
                               wtok=castW["rq"])

                        if g == 0:
                            loads.append((rkv_bf.ap().rearrange(
                                "(c p) e -> p c e", p=128), 512,
                                castW["rkv"]))

                            def fn_rkv(buf, pset):
                                sc.wait("tensor", tokRe)
                                sc.wait("tensor", tok_small)
                                prkf, prkl = pset[0], pset[1]
                                for dch2 in range(2):
                                    for c in range(8):
                                        sc.op("tensor", "matmul",
                                              prkf[:, 64 * dch2:64 * dch2 + 64],
                                              buf[:, c,
                                                  128 * dch2:128 * dch2 + 128],
                                              pooled_ab[:, c:512:8],
                                              start=(c == 0), stop=False)
                                    sc.op("tensor", "matmul",
                                          prkf[:, 64 * dch2:64 * dch2 + 64],
                                          rkvb[:, 128 * dch2:128 * dch2 + 128],
                                          onesb[:, 0:64],
                                          start=False, stop=True)
                                tokP = sc.fence("tensor")
                                sc.wait("vector", tokP)
                                sc.op("vector", "tensor_copy", rk_fm[:, :],
                                      prkf[:, 0:128])
                                for c in range(8):
                                    sc.op("tensor", "matmul", prkl[0:64, :],
                                          pooled_ab[:, c:512:8], buf[:, c, :],
                                          start=(c == 0), stop=False)
                                sc.op("tensor", "matmul", prkl[0:64, :],
                                      onescol[:, 0:64], rkvb[:, :],
                                      start=False, stop=True)
                                peF = sc.fence("tensor")
                                sc.wait("vector", peF)
                                sc.op("vector", "tensor_copy", rv_lm[0:64, :],
                                      prkl[0:64, 256:512])
                                sc.op("vector", "tensor_copy", rv_lm[64:128, :],
                                      prkl[0:64, 256:512])
                                evF = sc.fence("vector")
                                return peF, evF
                            comps.append(fn_rkv)

                        bufs = [wbuf, wbuf2]
                        tokL, peFs, evFs = {}, {}, {}

                        def issue(i):
                            if i - 2 in peFs:
                                sc.wait("sync", peFs[i - 2])
                            wsrc, ncw, wtok = loads[i]
                            if wtok is not None:
                                sc.wait("sync", wtok)
                            tokL[i] = sc.dma("sync",
                                             bufs[i % 2][:, :, 0:ncw], wsrc,
                                             "dx")
                        issue(0)
                        if len(loads) > 1:
                            issue(1)
                        if g == 0:
                            sc.wait("gpsimd", tokPool)
                            tokCI = sc.dma("gpsimd",
                                           cc_in.ap().rearrange("(p q) -> p q",
                                                                p=128),
                                           pooled_m[:, :], "dg")
                            sc.wait("gpsimd", tokCI)
                            sc.op("gpsimd", "collective_compute", "AllGather",
                                  ALU.bypass, [list(range(NCORE))],
                                  [cc_in.ap()], [cc_out.ap()])
                            tokCC = sc.fence("gpsimd", "cc")
                            sc.wait("gpsimd", tokCC)
                            tokRe = sc.dma(
                                "gpsimd",
                                pooled_ab[:, :].rearrange("p (r w) -> p r w",
                                                          r=8),
                                cc_out.ap().rearrange("r (p w) -> p r w",
                                                      p=128), "dg")
                            if "pooled" in dbg:
                                sc.wait("gpsimd", tokRe)
                                sc.dma("gpsimd", dbg["pooled"],
                                       pooled_ab[:, :], "dg")
                        for i, fn in enumerate(comps):
                            sc.wait("tensor", tokL[i])
                            if i - 2 in evFs:
                                sc.wait("tensor", evFs[i - 2])
                            peFs[i], evFs[i] = fn(
                                bufs[i % 2],
                                psl8[0:4] if i % 2 == 0 else psl8[4:8])
                            if i + 2 < len(loads):
                                issue(i + 2)
                        sc.wait("tensor", evFs[len(comps) - 1])
                        if len(comps) >= 2:
                            sc.wait("tensor", evFs[len(comps) - 2])
                    sc.barrier()
                    if stop == "prod%d" % g:
                        return early_finish()

                    prevTail = None
                    for si in range(nsu):
                        su = su0 + si
                        qc = 128 * si
                        if ("only%d" % su) not in skip and any(
                                s.startswith("only") for s in skip):
                            continue
                        if prevTail is not None:
                            sc.wait("scalar", prevTail["res"])
                            sc.wait("scalar", prevTail["ao"])
                            sc.wait("vector", prevTail["ao"])
                        with nc.psum_tensor(U("ps_o"), [128, 1024], F32) as ps_o, \
                             nc.psum_tensor(U("ps_sum"), [128, 128], F32) as ps_sum, \
                             nc.psum_tensor(U("ps_sc0"), [128, 1024], F32) as sc0, \
                             nc.psum_tensor(U("ps_sc1"), [128, 1024], F32) as sc1, \
                             nc.psum_tensor(U("ps_pt"), [128, 1024], BF16) as ps_pt:
                            scs = [sc0, sc1]
                            Ps = [P_sb, P2_sb]
                            tokSum = {}
                            tokSc, tokExp, tokPT, tokPTe, tokAV = {}, {}, {}, {}, {}

                            def emit_scores(r):
                                if r >= 2:
                                    sc.wait("tensor", tokExp[r - 2])
                                slot0 = su % 6
                                for hh in range(4):
                                    h = 4 * r + hh
                                    chq = h // 2
                                    bp = 64 * (h % 2)
                                    cb = 256 * hh
                                    if slot0 != 5:
                                        ro2 = (384 if su == 0 else
                                               (0 if su == 1 else 128))
                                        sc.op("tensor", "matmul",
                                              scs[r % 2][:, cb:cb + 256],
                                              q_grp[bp:bp + 64, chq, qc:qc + 128],
                                              k_ring[bp:bp + 64, chq,
                                                     128 * slot0:128 * slot0 + 256],
                                              start=True, stop=False)
                                        sc.op("tensor", "matmul",
                                              scs[r % 2][:, cb:cb + 256],
                                              onescol[:, :],
                                              negall[0:1, ro2:ro2 + 256],
                                              start=False, stop=True)
                                        continue
                                    for ks in range(2):
                                        slot = (su + ks) % 6
                                        cbk = cb + 128 * ks
                                        sc.op("tensor", "matmul",
                                              scs[r % 2][:, cbk:cbk + 128],
                                              q_grp[bp:bp + 64, chq, qc:qc + 128],
                                              k_ring[bp:bp + 64, chq,
                                                     128 * slot:128 * slot + 128],
                                              start=True, stop=False)
                                        sc.op("tensor", "matmul",
                                              scs[r % 2][:, cbk:cbk + 128],
                                              onescol[:, :],
                                              negall[0:1, 128:256],
                                              start=False, stop=True)
                                tokSc[r] = sc.fence("tensor")

                            emit_scores(0)
                            emit_scores(1)
                            for rnd in range(4):
                                sc.wait("scalar", tokSc[rnd])
                                if rnd >= 2:
                                    sc.wait("scalar", tokPT[rnd - 2])
                                    sc.wait("scalar", tokSum[rnd - 2])
                                if "exp" in skip:
                                    sc.op("scalar", "copy", junk[:, 0:1],
                                          epsv[:, :])
                                else:
                                    sc.op("scalar", "activation",
                                          Ps[rnd % 2][:, :],
                                          scs[rnd % 2][:, :], ACT.Exp,
                                          scale=0.125)
                                tokExp[rnd] = sc.fence("scalar")
                                sc.wait("vector", tokExp[rnd])
                                if "tri" in skip or "exp" in skip:
                                    sc.op("vector", "tensor_copy", junk[:, 1:2],
                                          epsv[:, :])
                                else:
                                    sc.op("vector", "tensor_mul",
                                          Ps[rnd % 2][:, :],
                                          Ps[rnd % 2][:, :], tri[:, :])
                                tokTri = sc.fence("vector")
                                sc.wait("tensor", tokTri)
                                for hh in range(4):
                                    for ks in range(2):
                                        cb = 256 * hh + 128 * ks
                                        sc.op("tensor", "transpose",
                                              ps_pt[:, cb:cb + 128],
                                              Ps[rnd % 2][:, cb:cb + 128],
                                              ident[:, :])
                                tokPT[rnd] = sc.fence("tensor")
                                sc.wait("vector", tokPT[rnd])
                                if rnd >= 1:
                                    sc.wait("vector", tokAV[rnd - 1])
                                sc.op("vector", "tensor_copy", PT_sb[:, :],
                                      ps_pt[:, :])
                                tokPTe[rnd] = sc.fence("vector")
                                sc.wait("tensor", tokPTe[rnd])
                                if rnd == 0 and prevTail is not None:
                                    sc.wait("tensor", prevTail["res"])
                                    sc.wait("tensor", prevTail["oev"])
                                for hh in range(4):
                                    h = 4 * rnd + hh
                                    for ks in range(2):
                                        slot = (su + ks) % 6
                                        cb = 256 * hh + 128 * ks
                                        sc.op("tensor", "matmul",
                                              ps_o[64 * (h % 2):64 * (h % 2) + 64,
                                                   128 * (h // 2):128 * (h // 2) + 128],
                                              v_ring[:, slot, 64 * h:64 * h + 64],
                                              PT_sb[:, cb:cb + 128],
                                              start=(ks == 0), stop=(ks == 1))
                                tokAV[rnd] = sc.fence("tensor")
                                for hh in range(4):
                                    h = 4 * rnd + hh
                                    sc.op("vector", "tensor_scalar",
                                          latT_sb[:, 0:256],
                                          Ps[rnd % 2][:, 256 * hh:256 * hh + 256],
                                          1.0, 0.0, ALU.mult, op1=ALU.add,
                                          accum_out=sums_v[:, h:h + 1])
                                tokSum[rnd] = sc.fence("vector")
                                if rnd + 2 < 4:
                                    emit_scores(rnd + 2)
                            vgap()
                            tokSumF = sc.fence("vector")
                            sc.wait("tensor", tokSumF)
                            sc.wait("tensor", tokIdf)
                            sc.op("tensor", "transpose", ps_sum[0:16, :],
                                  sums_v[:, :], identf[:, :])
                            tokTrS = sc.fence("tensor")
                            sc.wait("scalar", tokAV[3])
                            sc.op("scalar", "copy", O_sb[:, :], ps_o[:, :])
                            sc.wait("scalar", tokTrS)
                            sc.op("scalar", "activation", gsum[:, :],
                                  ps_sum[0:16, :], ACT.Identity,
                                  bias=tinyv[0:16, :])
                            tokOev = sc.fence("scalar")
                        sc.wait("vector", tokOev)
                        sc.wait("tensor", tokOev)
                        if stop == "su%d:loc" % su:
                            return early_finish()
                        if "osb" in dbg:
                            sc.wait("gpsimd", tokOev)
                            sc.dma("gpsimd",
                                   dbg["osb"][:, 1024 * su:1024 * su + 1024],
                                   O_sb[:, :], "dg")
                        with nc.psum_tensor(U("ps_lsc"), [128, 256], F32) as ps_lsc:
                            for hp in range(4):
                                bp = 64 * (hp % 2)
                                sc.op("tensor", "matmul",
                                      ps_lsc[:, 64 * hp:64 * hp + 64],
                                      rq_grp[bp:bp + 64, hp // 2, qc:qc + 128],
                                      rk_fm[bp:bp + 64,
                                            64 * (hp // 2):64 * (hp // 2) + 64],
                                      start=True, stop=False)
                                sc.op("tensor", "matmul",
                                      ps_lsc[:, 64 * hp:64 * hp + 64],
                                      onescol[:, :],
                                      latneg[0:1, 64 * (0 if su == 0 else 1 + (su - 1) // 2):64 * (0 if su == 0 else 1 + (su - 1) // 2) + 64],
                                      start=False, stop=True)
                            tokL = sc.fence("tensor")
                            sc.wait("scalar", tokL)
                            for hp in range(4):
                                sc.op("scalar", "activation",
                                      lat_sb[:, 64 * hp:64 * hp + 64],
                                      ps_lsc[:, 64 * hp:64 * hp + 64], ACT.Exp,
                                      scale=0.125, accum_out=lsum[:, hp:hp + 1])
                            sc.op("scalar", "copy", junk[:, 2:3], epsv[:, :])
                            tokL = sc.fence("scalar")
                        sc.wait("vector", tokL)
                        sc.op("vector", "tensor_scalar", lsum[:, :], lsum[:, :],
                              TINY, None, ALU.add)
                        vgap()
                        sc.op("vector", "reciprocal", linv[:, :], lsum[:, :])
                        vgap()
                        tokLi = sc.fence("vector")
                        sc.wait("scalar", tokLi)
                        for hp in range(4):
                            sc.op("scalar", "activation",
                                  lat_sb[:, 64 * hp:64 * hp + 64],
                                  lat_sb[:, 64 * hp:64 * hp + 64], ACT.Identity,
                                  scale=linv[:, hp:hp + 1])
                        tokLn = sc.fence("scalar")
                        sc.wait("tensor", tokLn)
                        with nc.psum_tensor(U("ps_lpt"), [128, 256], BF16) as ps_lpt:
                            for tp in range(2):
                                sc.op("tensor", "transpose",
                                      ps_lpt[:, 128 * tp:128 * tp + 128],
                                      lat_sb[:, 128 * tp:128 * tp + 128],
                                      ident[:, :])
                            tokP = sc.fence("tensor")
                            sc.wait("vector", tokP)
                            sc.op("vector", "tensor_copy", latT_sb[:, :],
                                  ps_lpt[:, :])
                            tokE = sc.fence("vector")
                            sc.wait("tensor", tokE)
                        with nc.psum_tensor(U("ps_lo"), [128, 256], F32) as ps_lo:
                            for hp in range(4):
                                bp = 64 * (hp % 2)
                                sc.op("tensor", "matmul",
                                      ps_lo[bp:bp + 64,
                                            128 * (hp // 2):128 * (hp // 2) + 128],
                                      rv_lm[bp:bp + 64, 64 * hp:64 * hp + 64],
                                      latT_sb[bp:bp + 64,
                                              128 * (hp // 2):128 * (hp // 2) + 128],
                                      start=True, stop=False)
                                sc.op("tensor", "matmul",
                                      ps_lo[bp:bp + 64,
                                            128 * (hp // 2):128 * (hp // 2) + 128],
                                      zrow[0:1, 0:64], onescol[0:1, :],
                                      start=False, stop=True)
                            tokP = sc.fence("tensor")
                            sc.wait("vector", tokP)
                            sc.op("vector", "tensor_copy", lat_sb[:, :],
                                  ps_lo[:, :])
                            tokE = sc.fence("vector")
                            sc.wait("tensor", tokE)
                        if stop == "su%d:lat" % su:
                            return early_finish()
                        with nc.psum_tensor(U("ps_r2"), [128, 1024], F32) as ps_r2:
                            sc.wait("tensor", tokRoutw)
                            for dch in range(8):
                                for a in range(2):
                                    sc.op("tensor", "matmul",
                                          ps_r2[:, 128 * dch:128 * dch + 128],
                                          routw[:, a, 128 * dch:128 * dch + 128],
                                          lat_sb[:, 128 * a:128 * a + 128],
                                          start=(a == 0), stop=False)
                                sc.op("tensor", "matmul",
                                      ps_r2[:, 128 * dch:128 * dch + 128],
                                      routb[:, 128 * dch:128 * dch + 128],
                                      onesb[:, 0:128], start=False, stop=True)
                            tokP = sc.fence("tensor")
                            sc.wait("scalar", tokP)
                            sc.op("scalar", "copy", R_sb[:, :], ps_r2[:, :])
                            tokRe2 = sc.fence("scalar")
                        sc.wait("tensor", tokRe2)
                        if stop == "su%d:r" % su:
                            return early_finish()
                        if "rsb" in dbg:
                            sc.wait("gpsimd", tokRe2)
                            sc.dma("gpsimd",
                                   dbg["rsb"][:, 1024 * su:1024 * su + 1024],
                                   R_sb[:, :], "dg")
                        with nc.psum_tensor(U("ps_g"), [128, 128], F32) as ps_g:
                            for c in range(8):
                                sc.op("tensor", "matmul", ps_g[0:16, :],
                                      glG[:, 16 * c:16 * c + 16],
                                      O_sb[:, 128 * c:128 * c + 128],
                                      start=(c == 0), stop=(c == 7))
                            for c in range(8):
                                sc.op("tensor", "matmul", ps_g[32:48, :],
                                      grG[:, 16 * c:16 * c + 16],
                                      R_sb[:, 128 * c:128 * c + 128],
                                      start=(c == 0), stop=(c == 7))
                            tokP = sc.fence("tensor")
                            sc.wait("vector", tokP)
                            sc.op("vector", "reciprocal", ginv[:, :],
                                  gsum[:, :])
                            vgap()
                            sc.op("vector", "tensor_mul", gg[:, :],
                                  ps_g[0:16, :], ginv[:, :])
                            vgap()
                            sc.op("vector", "tensor_add", gg[:, :], gg[:, :],
                                  ps_g[32:48, :])
                            vgap()
                            tokGv = sc.fence("vector")
                        sc.wait("tensor", tokGv)
                        sc.wait("scalar", tokGv)
                        sc.op("scalar", "activation", gg[:, :], gg[:, :],
                              ACT.Tanh, scale=0.5)
                        tokGs = sc.fence("scalar")
                        sc.wait("vector", tokGs)
                        sc.op("vector", "tensor_scalar", om_bf[:, :], gg[:, :],
                              -0.5, 0.5, ALU.mult, op1=ALU.add)
                        sc.op("vector", "tensor_scalar", gg[:, :], gg[:, :],
                              0.5, 0.5, ALU.mult, op1=ALU.add)
                        vgap()
                        sc.op("vector", "tensor_mul", gp_bf[:, :], gg[:, :],
                              ginv[:, :])
                        tokGb = sc.fence("vector")
                        sc.wait("tensor", tokGb)
                        if stop == "su%d:g" % su:
                            return early_finish()
                        if "gate" in dbg:
                            sc.wait("sync", tokGs)
                            sc.dma("sync", dbg["gate"][:, 128 * su:128 * su + 128],
                                   gg[:, :], "st")
                        with nc.psum_tensor(U("ps_bc"), [128, 1024], F32) as ps_bc:
                            for c in range(8):
                                sc.op("tensor", "matmul",
                                      ps_bc[:, 128 * c:128 * c + 128],
                                      ebc[:, 128 * c:128 * c + 128],
                                      gp_bf[:, :], start=True, stop=False)
                                sc.op("tensor", "matmul",
                                      ps_bc[:, 128 * c:128 * c + 128],
                                      zrow[0:1, :], onescol[0:1, :],
                                      start=False, stop=True)
                            tokP = sc.fence("tensor")
                            sc.wait("vector", tokP)
                            sc.op("vector", "tensor_mul", mix_sb[:, :],
                                  ps_bc[:, :], O_sb[:, :])
                            tokM1 = sc.fence("vector")
                            sc.wait("tensor", tokM1)
                            for c in range(8):
                                sc.op("tensor", "matmul",
                                      ps_bc[:, 128 * c:128 * c + 128],
                                      ebc[:, 128 * c:128 * c + 128],
                                      om_bf[:, :], start=True, stop=False)
                                sc.op("tensor", "matmul",
                                      ps_bc[:, 128 * c:128 * c + 128],
                                      zrow[0:1, :], onescol[0:1, :],
                                      start=False, stop=True)
                            tokP = sc.fence("tensor")
                            sc.wait("vector", tokP)
                            sc.op("vector", "tensor_mul", P_sb[:, :],
                                  ps_bc[:, :], R_sb[:, :])
                            sc.op("vector", "tensor_add", mix_sb[:, :],
                                  mix_sb[:, :], P_sb[:, :])
                            tokMx = sc.fence("vector")
                        sc.wait("tensor", tokMx)
                        with nc.psum_tensor(U("ps_ao"), [128, 1024], F32) as ps_ao:
                            sc.wait("tensor", tokOutw)
                            for half in range(2):
                                for c in range(8):
                                    sc.op("tensor", "matmul",
                                          ps_ao[:, 512 * half:512 * half + 512],
                                          mix_sb[:, 128 * c:128 * c + 128],
                                          outw[:, c, 512 * half:512 * half + 512],
                                          start=(c == 0), stop=False)
                                sc.op("tensor", "matmul",
                                      ps_ao[:, 512 * half:512 * half + 512],
                                      onescol[:, :],
                                      outb[:, 512 * half:512 * half + 512],
                                      start=False, stop=True)
                            tokAoPE = sc.fence("tensor")
                            sc.wait("vector", tokAoPE)
                            sc.op("vector", "tensor_add", x_sb[:, su + 1, :],
                                  x_sb[:, su + 1, :], ps_ao[:, :])
                            tokRes = sc.fence("vector")
                        if si == nsu - 1:
                            sc.wait("tensor", tokRes)
                        if su == 0:
                            sc.wait("scalar", tokRes)
                            sc.op("scalar", "activation", x_sb[:, 1, :],
                                  x_sb[:, 1, :], ACT.Identity,
                                  scale=hkeep[:, :])
                            sc.fence("scalar")
                        if si == nsu - 1:
                            sc.barrier()
                            prevTail = None
                        else:
                            prevTail = {"res": tokRes, "ao": tokAoPE,
                                        "oev": tokOev}
                        if stop == "su%d" % su:
                            return early_finish()
        if "x2" in dbg:
            sc.dma("sync", dbg["x2"].rearrange("(s p) d -> p s d", p=128),
                   x_sb[:, :, :], "st")
            sc.barrier()

        with nc.sbuf_tensor("yw3", [128, 8, 2305], BF16) as yw3:
            sc.op("vector", "memset", yw3[:, :, 0:1], 0.0)
            ln_sweep(list(range(1, NT)), yw3, 1)
            if "y3fm" in dbg:
                sc.dma("gpsimd", dbg["y3fm"].rearrange("p (c t) -> p c t", c=8),
                       yw3[:, :, :], "dg")
                sc.barrier()
            if stop == "ln3":
                return early_finish()
            with nc.sbuf_tensor("wt_sb2", [128, 24, 1024], BF16) as wt_sb2:
                ffn_sweep(yw3, 129, 2048, 2, wt_sb2, store=True)

        sc.wait_queue("sync", "st")

        with nc.Block() as block:
            sc.replay(block)

    return nc


def _bf(a):
    return np.asarray(a, dtype=BF)


def make_consts(core):
    o = {}
    o["ident"] = _bf(np.eye(128, dtype=np.float32))
    o["identf"] = np.eye(128, dtype=np.float32)
    i = np.arange(128)[:, None]
    j = np.arange(256)[None, :]
    tri1 = ((j >= i) & (j < i + 128)).astype(np.float32)
    o["tri"] = _bf(np.tile(tri1, (1, 4)))
    es = np.zeros((128, 256), np.float32)
    for h in range(16):
        es[:, 16 * h + h] = 1.0
    o["esum"] = _bf(es)
    eb = np.zeros((16, 1024), np.float32)
    for c in range(8):
        for p in range(128):
            eb[2 * c + p // 64, 128 * c + p] = 1.0
    o["ebc"] = _bf(eb)
    nr = np.zeros((1, 640), np.float32)
    if core == 0:
        nr[0, 0:128] = NEG
        nr[0, 384:640] = NEG
    o["negall"] = _bf(nr)
    ln = np.zeros((1, 576), np.float32)
    l = np.arange(64)
    for ridx in range(9):
        if ridx == 0:
            m = 8 * core - 1 if core > 0 else 0
        else:
            m = 8 * core + (ridx - 1)
        ln[0, 64 * ridx:64 * ridx + 64] = np.where(l >= m, NEG, 0.0)
    o["latneg"] = _bf(ln)
    o["hkeep"] = np.full((128, 1), 0.0 if core == 0 else 1.0, np.float32)
    return o


def kernel(L0_attn_x, L0_attn_qkv_w, L0_attn_qkv_b, L0_attn_rq_w, L0_attn_rq_b,
           L0_attn_rkv_w, L0_attn_rkv_b, L0_attn_rout_w, L0_attn_rout_b,
           L0_attn_out_w, L0_attn_out_b, L0_attn_gl_w, L0_attn_gr_w,
           L0_ffn_W_r, L0_ffn_W_k, L0_ffn_W_v, L0_ffn_mu_r, L0_ffn_mu_k):
    x = np.ascontiguousarray(np.asarray(L0_attn_x, dtype=np.float32)[0])
    S = x.shape[0]
    xp = np.concatenate([np.zeros((256, D), np.float32), x], axis=0)

    shared = {
        "qkv_w": np.ascontiguousarray(L0_attn_qkv_w, dtype=np.float32),
        "qkv_b": np.asarray(L0_attn_qkv_b, np.float32).reshape(1, 3 * D),
        "rq_w": np.ascontiguousarray(L0_attn_rq_w, dtype=np.float32),
        "rq_b": np.asarray(L0_attn_rq_b, np.float32).reshape(1, 256),
        "rkv_w": np.ascontiguousarray(L0_attn_rkv_w, dtype=np.float32),
        "rkv_b": np.asarray(L0_attn_rkv_b, np.float32).reshape(1, 512),
        "rout_w": np.ascontiguousarray(L0_attn_rout_w, dtype=np.float32),
        "rout_b": np.asarray(L0_attn_rout_b, np.float32).reshape(1, D),
        "out_w": np.ascontiguousarray(L0_attn_out_w, dtype=np.float32),
        "out_b": np.asarray(L0_attn_out_b, np.float32).reshape(1, D),
        "gl_w": np.asarray(L0_attn_gl_w, np.float32).reshape(64, 1),
        "gr_w": np.asarray(L0_attn_gr_w, np.float32).reshape(64, 1),
        "W_r": np.ascontiguousarray(L0_ffn_W_r, dtype=np.float32),
        "W_k": np.ascontiguousarray(L0_ffn_W_k, dtype=np.float32),
        "W_v": np.ascontiguousarray(L0_ffn_W_v, dtype=np.float32),
        "mu_r": np.asarray(L0_ffn_mu_r, np.float32).reshape(8, 128).T.copy(),
        "mu_k": np.asarray(L0_ffn_mu_k, np.float32).reshape(8, 128).T.copy(),
    }
    in_maps = []
    for core in range(NCORE):
        m = dict(shared)
        m["x"] = np.ascontiguousarray(
            xp[2048 * core:2048 * core + NT * 128, :])
        m.update(make_consts(core))
        in_maps.append(m)

    nc = build_program(DEBUG_TAPS)
    res = run_bass_kernel_spmd(nc, in_maps, core_ids=list(range(NCORE)))
    out = np.concatenate([res.results[i]["out"] for i in range(NCORE)], axis=0)
    kernel.last_results = res
    return out.reshape(1, S, D)
